# version 05
# baseline (speedup 1.0000x reference)
import math
import numpy as np
from contextlib import ExitStack

import concourse.bass as bass
import concourse.mybir as mybir
from concourse.bass_utils import run_bass_kernel_spmd

F32 = mybir.dt.float32
BF16 = mybir.dt.bfloat16
AF = mybir.ActivationFunctionType
ALU = mybir.AluOpType
AX = mybir.AxisListType

NCORES = 8
SEQ = 2048
D = 1024
HID = 4096
H = 8
WIN = 16
TPS = 16
NG = 8
ASC = 1.0 / math.sqrt(128.0)
LNEPS = 1e-5
RMSEPS = 1e-6

ENGS = ("sync", "gpsimd", "tensor", "vector", "scalar")

PHASE_MASK = [True] * 6
DEBUG_TAP = None
DEBUG_MAX_TILES = None
DEBUG_TAPB = False


class Walk:

    def __init__(self, engname, eng, sems):
        self.engname = engname
        self.eng = eng
        self.sems = sems
        self.n = {k: 0 for k in sems}
        self.mark = {}
        self.barn = 0

    def do(self, who, fn, inc=None, waits=()):
        if self.engname == who:
            for sem, val in waits:
                if val:
                    self.eng.wait_ge(self.sems[sem], val)
            instr = fn(self.eng)
        else:
            instr = None
        if inc is not None:
            sem, amt = inc
            self.n[sem] += amt
            if instr is not None:
                instr.then_inc(self.sems[sem], amt)

    def setmark(self, key, sem):
        self.mark[key] = self.n[sem]
        return self.mark[key]

    def barrier(self):
        self.barn += 1
        for who in ("gpsimd", "tensor", "vector", "scalar"):
            self.do(who, lambda e: e.nop(), inc=("cB", 1))
        tgt = self.barn * 4
        if self.engname in ENGS:
            self.eng.wait_ge(self.sems["cB"], tgt)


def build_walk(w, t):
    v = dict(
        xng_tp=None,
        xnTg_mm=None,
        pso_ev=None,
        gelu_bank=[None, None],
        w1done=None, w2=[None] * 8,
        last_tp=None,
    )

    w.do("vector", lambda e: e.memset(t["onesb"][:], 1.0), inc=("cV", 1))
    w.setmark("ones", "cV")
    w.do("gpsimd", lambda e: e.dma_start(out=t["b1tmp"][:], in_=t["b1r_d"][:]),
         inc=("dW", 16))
    w.setmark("b1tmp", "dW")
    w.do("gpsimd", lambda e: e.dma_start(out=t["b2row"][:], in_=t["b2r_d"][:]),
         inc=("dW", 16))
    w.setmark("b2row", "dW")
    w.do("sync", lambda e: e.dma_start(out=t["b1T"][:], in_=t["b1tmp"][:],
                                       transpose=True),
         inc=("dT", 16), waits=[("dW", w.mark["b1tmp"])])
    w.setmark("b1T", "dT")
    w.do("scalar", lambda e: e.copy(t["b1f"][:], t["b1T"][:]),
         inc=("cS", 1), waits=[("dT", w.mark["b1T"])])
    xin = t["x_d"].rearrange("s (tt p) d -> s p tt d", p=128)
    xres3 = t["xres"].rearrange("p (tt d) -> p tt d", d=1024)
    for i in range(4):
        w.do("sync",
             lambda e, i=i: e.dma_start(out=xres3[:, i * 4:(i + 1) * 4, :],
                                        in_=xin[0, :, i * 4:(i + 1) * 4, :]),
             inc=("dX", 16))
        w.setmark(("xload", 0, i), "dX")
    w.do("gpsimd", lambda e: e.dma_start(out=t["idb"][:], in_=t["eye_d"][:]),
         inc=("dW", 16))
    w.setmark("idb", "dW")
    wnames = ("wq_d", "wk_d", "wv_d", "wo_d")
    pst = t["pst"]
    for i in range(4):
        wsrc = t[wnames[i]].rearrange("(oc p) d -> p oc d", p=128)
        wstg3 = t["wstg"].rearrange("p (oc d) -> p oc d", oc=8)
        guard = [("cP", w.mark.get(("wtp", i - 1, 15)))]
        w.do("gpsimd", lambda e, wsrc=wsrc, wstg3=wstg3: e.dma_start(
            out=wstg3[:], in_=wsrc[:]), inc=("dW", 16), waits=guard)
        w.setmark(("wstgs", i), "dW")
        for grp in range(16):
            dd, och = grp // 2, (grp % 2) * 4
            tpw = []
            if grp == 0:
                tpw = [("dW", w.mark[("wstgs", i)]), ("dW", w.mark["idb"])]
            cpm = w.mark.get(("wcp", i, grp - 2)) if grp >= 2 else \
                w.mark.get(("wcp", i - 1, 14 + grp))
            if cpm:
                tpw = tpw + [("cV", cpm)]
            for b in range(4):
                oc = och + b
                w.do("tensor", lambda e, dd=dd, oc=oc, b=b, grp=grp: e.transpose(
                    pst[grp % 2][:, b * 128:(b + 1) * 128],
                    t["wstg"][:, oc * 1024 + dd * 128:oc * 1024 + dd * 128 + 128],
                    t["idb"][:]),
                    inc=(("cP", 1) if b == 3 else None),
                    waits=tpw if b == 0 else [])
            w.setmark(("wtp", i, grp), "cP")
            cw = [("cP", w.mark[("wtp", i, grp)])]
            if grp == 0 and i > 0:
                cw.append(("dS", w.mark[("wTout", i - 1)]))
            w.do("vector", lambda e, dd=dd, och=och, grp=grp: e.tensor_copy(
                t["wqT"][:, dd * 1024 + (och // 4) * 512:
                         dd * 1024 + (och // 4) * 512 + 512],
                pst[grp % 2][:]),
                inc=("cV", 1), waits=cw)
            w.setmark(("wcp", i, grp), "cV")
        w.do("sync", lambda e, i=i: e.dma_start(
            out=t[f"wt_dram{i}"][:], in_=t["wqT"][:]),
            inc=("dS", 16), waits=[("cV", w.mark[("wcp", i, 15)])])
        w.setmark(("wTout", i), "dS")

    full = all(PHASE_MASK)
    xres3 = t["xres"].rearrange("p (tt d) -> p tt d", d=1024)
    outv = t["out_d"].rearrange("s (tt p) d -> s p tt d", p=128)
    for seq in range(2):
        if PHASE_MASK[seq * 3 + 0]:
            phase_ffn(w, t, v, seq, phid=seq * 3 + 0,
                      load_weights=(seq == 0) if full else True, store=False)
        if PHASE_MASK[seq * 3 + 1]:
            w.barrier()
            phase_attn(w, t, v, seq, phid=seq * 3 + 1)
            w.barrier()
        if PHASE_MASK[seq * 3 + 2]:
            phase_ffn(w, t, v, seq, phid=seq * 3 + 2,
                      load_weights=True, store=True)
        else:
            cv_now = w.n["cV"]
            w.do("sync", lambda e, seq=seq: e.dma_start(
                out=outv[seq], in_=xres3[:]),
                inc=("dS", 16), waits=[("cV", cv_now)])
            if seq == 0:
                w.do("sync", lambda e: e.dma_start(out=xres3[:], in_=xin[1]),
                     inc=("dX", 16))
                for g in range(NG):
                    w.setmark(("xload", 1, g), "dX")

    total_dS = w.n["dS"]
    if w.engname == "sync":
        w.eng.wait_ge(w.sems["dS"], total_dS)



def phase_ffn(w, t, v, seq, phid, load_weights, store):
    xres = t["xres"]
    xres3 = xres.rearrange("p (tt d) -> p tt d", d=1024)
    xng, xnTg, hbuf = t["xng"], t["xnTg"], t["hbuf"]
    w1b, w2b = t["w1b"], t["w2b"]
    psh, pso = t["psh"], t["pso"]
    xin = t["x_d"].rearrange("s (tt p) d -> s p tt d", p=128)
    outv = t["out_d"].rearrange("s (tt p) d -> s p tt d", p=128)

    if load_weights and phid == 0:
        guard = [("dS", w.mark[("wTout", 3)])]
        for k in range(8):
            w.do("gpsimd",
                 lambda e, k=k: e.dma_start(out=w1b[:, k * 4096:(k + 1) * 4096],
                                            in_=t["w1_d"][k * 128:(k + 1) * 128, :]),
                 inc=("dW", 16), waits=guard if k == 0 else [])
        v["w1done"] = w.setmark(("w1", phid), "dW")
        w2v = t["w2_d"].rearrange("(j c p) o -> j p c o", j=8, p=128)
        w2b4 = w2b.rearrange("p (j c o) -> p j c o", j=8, c=4)
        for j in range(8):
            w.do("gpsimd",
                 lambda e, j=j: e.dma_start(out=w2b4[:, j], in_=w2v[j]),
                 inc=("dW", 16))
            v["w2"][j] = w.setmark(("w2", phid, j), "dW")
    elif load_weights:
        guard = [("dT", v["last_tp"]), ("dS", w.mark["wbfout"])]
        w1v = w1b.rearrange("p (k m) -> p k m", k=8)
        w1dv = t["w1bf_dram"].rearrange("p (k m) -> p k m", k=8)
        for j in range(4):
            w.do("gpsimd", lambda e, j=j: e.dma_start(
                out=w1v[:, :, j * 1024:(j + 1) * 1024],
                in_=w1dv[:, :, j * 1024:(j + 1) * 1024]),
                inc=("dW", 16), waits=guard if j == 0 else [])
            w.setmark(("w1s", phid, j), "dW")
        v["w1done"] = w.mark[("w1s", phid, 0)]
        v["w1slice"] = [w.mark[("w1s", phid, j)] for j in range(4)]
        for j in range(4):
            w.do("gpsimd", lambda e, j=j: e.dma_start(
                out=w2b[:, j * 8192:(j + 1) * 8192],
                in_=t["w2bf_dram"][:, j * 8192:(j + 1) * 8192]),
                inc=("dW", 16))
            w.setmark(("w2s", phid, j), "dW")
            for jj in range(2):
                v["w2"][j * 2 + jj] = w.mark[("w2s", phid, j)]

    for tt in range(TPS):
        xt = xres3[:, tt, :]
        wv, ws = [], []
        if phid == 0:
            wv = [("dX", w.mark[("xload", 0, tt // 4)])]
            ws = list(wv)
        elif phid == 3:
            wv = [("dX", w.mark[("xload", 1, tt // 2)])]
            ws = list(wv)
        if tt == 0:
            ws = ws + [("dT", v["xng_tp"])]
        w.do("vector", lambda e, xt=xt, tt=tt: e.reduce_sum(
            t["xsumA"][:, tt:tt + 1], xt, axis=AX.X), inc=("cV", 1), waits=wv)
        w.do("scalar", lambda e, xt=xt, tt=tt: e.activation(
            out=xng[:, 0:1024], in_=xt, func=AF.Square,
            accum_out=t["xsqA"][:, tt:tt + 1]), inc=("cS", 1), waits=ws)
    w.do("scalar", lambda e: e.copy(t["xsqAe"][:], t["xsqA"][:]),
         inc=("cS", 1), waits=[("cS", w.n["cS"])])
    w.setmark(("a1s", phid), "cS")
    w.do("vector", lambda e: e.tensor_scalar_mul(t["muA"][:], t["xsumA"][:], 1.0 / D),
         inc=("cV", 1), waits=[("cS", w.mark[("a1s", phid)]), ("cV", w.n["cV"])])
    w.do("vector", lambda e: e.tensor_scalar(
        out=t["vtmpA"][:], in0=t["xsqAe"][:], scalar1=1.0 / D, scalar2=LNEPS,
        op0=ALU.mult, op1=ALU.add), inc=("cV", 1))
    w.do("vector", lambda e: e.tensor_mul(t["vtmp2A"][:], t["muA"][:], t["muA"][:]),
         inc=("cV", 1), waits=[("cV", w.n["cV"])])
    w.do("vector", lambda e: e.tensor_sub(t["vtmpA"][:], t["vtmpA"][:], t["vtmp2A"][:]),
         inc=("cV", 1), waits=[("cV", w.n["cV"])])
    w.setmark(("varA", phid), "cV")
    w.do("scalar", lambda e: e.activation(out=t["vtmp2A"][:], in_=t["vtmpA"][:],
                                          func=AF.Ln),
         inc=("cS", 1), waits=[("cV", w.mark[("varA", phid)])])
    w.do("scalar", lambda e: e.activation(out=t["rstdA"][:], in_=t["vtmp2A"][:],
                                          func=AF.Exp, scale=-0.5), inc=("cS", 1),
         waits=[("cS", w.n["cS"])])
    w.setmark(("rstdA", phid), "cS")

    xn_w = xng.rearrange("p (k two f) -> p k two f", k=8, two=2, f=128)
    for g in range(NG):
        for sub in range(2):
            tt = 2 * g + sub
            waits = []
            if sub == 0:
                waits.append(("cS", w.mark[("rstdA", phid)]))
                waits.append(("dT", v["xng_tp"]))
                waits.append(("cV", w.n["cV"]))
            w.do("vector", lambda e, tt=tt, sub=sub: e.tensor_scalar(
                out=xn_w[:, :, sub, :], in0=xres3[:, tt, :],
                scalar1=t["muA"][:, tt:tt + 1], scalar2=t["rstdA"][:, tt:tt + 1],
                op0=ALU.subtract, op1=ALU.mult), inc=("cV", 1), waits=waits)
        w.setmark(("Axn", phid, g), "cV")
        tw = [("cV", w.mark[("Axn", phid, g)]), ("cP", v["xnTg_mm"])]
        w.do("sync", lambda e: e.dma_start(
            out=xnTg.rearrange("p (b q) -> p b q", b=16),
            in_=xng[:], transpose=True), inc=("dT", 16), waits=tw)
        v["xng_tp"] = w.setmark(("AxnT", phid, g), "dT")
        v["last_tp"] = v["xng_tp"]
        if g > 0:
            emit_ffn_evac(w, t, v, seq, phid, g - 1, store, outv, xres3, xin)

        for m in range(32):
            pw = [("cS", v["gelu_bank"][m % 2])]
            if m == 0:
                pw.append(("dT", w.mark[("AxnT", phid, g)]))
            if g == 0:
                sl = v.get("w1slice")
                if sl and m % 8 == 0:
                    pw.append(("dW", sl[m // 8]))
                elif m == 0:
                    pw.append(("dW", v["w1done"]))
            bank = psh[m % 2]
            for k in range(8):
                w.do("tensor", lambda e, k=k, m=m, bank=bank: e.matmul(
                    bank[:, 0:256],
                    w1b[:, k * 4096 + m * 128:k * 4096 + (m + 1) * 128],
                    xnTg[:, k * 256:(k + 1) * 256],
                    start=(k == 0), stop=(k == 7)),
                    inc=(("cP", 1) if k == 7 else None),
                    waits=pw if k == 0 else [])
            w.setmark(("Ah", phid, g, m), "cP")
            w.do("scalar", lambda e, m=m, bank=bank: e.activation(
                out=hbuf[:, (m % 3) * 256:(m % 3 + 1) * 256], in_=bank[:, 0:256],
                func=AF.Gelu, bias=t["b1f"][:, m:m + 1]),
                inc=("cS", 1), waits=[("cP", w.mark[("Ah", phid, g, m)])])
            v["gelu_bank"][m % 2] = w.setmark(("Agelu", phid, g, m), "cS")
            if m >= 1:
                emit_mm2(w, t, v, phid, g, m - 1)
        emit_mm2(w, t, v, phid, g, 31)
        v["xnTg_mm"] = w.mark[("Ah", phid, g, 31)]
        for sub in range(2):
            for nh in range(2):
                w.do("tensor", lambda e, sub=sub, nh=nh: e.matmul(
                    pso[:, sub * 1024 + nh * 512:sub * 1024 + nh * 512 + 512],
                    t["onesb"][:], t["b2row"][:, nh * 512:nh * 512 + 512],
                    start=False, stop=True),
                    inc=(("cP", 1) if (sub, nh) == (1, 1) else None),
                    waits=[("dW", w.mark["b2row"]), ("cV", w.mark["ones"])]
                    if (sub, nh) == (0, 0) else [])
        w.setmark(("Aout", phid, g), "cP")
    emit_ffn_evac(w, t, v, seq, phid, NG - 1, store, outv, xres3, xin)
    if load_weights and phid == 0:
        w.do("sync", lambda e: e.dma_start(out=t["w1bf_dram"][:], in_=w1b[:]),
             inc=("dS", 16), waits=[("dW", v["w1done"])])
        w.do("sync", lambda e: e.dma_start(out=t["w2bf_dram"][:], in_=w2b[:]),
             inc=("dS", 16), waits=[("dW", v["w2"][7])])
        w.setmark("wbfout", "dS")


def emit_mm2(w, t, v, phid, g, m):
    pso, hbuf, w2b = t["pso"], t["hbuf"], t["w2b"]
    pw = [("cS", w.mark[("Agelu", phid, g, m)])]
    if m == 0:
        pw.append(("cV", v["pso_ev"]))
    if g == 0 and v["w2"][m // 4]:
        pw.append(("dW", v["w2"][m // 4]))
    for sub in range(2):
        for nh in range(2):
            w.do("tensor", lambda e, sub=sub, nh=nh, m=m: e.matmul(
                pso[:, sub * 1024 + nh * 512:sub * 1024 + nh * 512 + 512],
                hbuf[:, (m % 3) * 256 + sub * 128:(m % 3) * 256 + sub * 128 + 128],
                w2b[:, m * 1024 + nh * 512:m * 1024 + nh * 512 + 512],
                start=(m == 0), stop=False),
                waits=pw if (sub, nh) == (0, 0) else [])


def emit_ffn_evac(w, t, v, seq, phid, g, store, outv, xres3, xin):
    pso = t["pso"]
    for sub in range(2):
        tt = 2 * g + sub
        w.do("vector", lambda e, tt=tt, sub=sub: e.scalar_tensor_tensor(
            out=xres3[:, tt, :], in0=pso[:, sub * 1024:(sub + 1) * 1024],
            scalar=0.5, in1=xres3[:, tt, :],
            op0=ALU.mult, op1=ALU.add),
            inc=("cV", 1),
            waits=[("cP", w.mark[("Aout", phid, g)])] if sub == 0 else [])
    v["pso_ev"] = w.setmark(("Aevac", phid, g), "cV")
    if store:
        w.do("sync", lambda e, g=g: e.dma_start(
            out=outv[seq, :, 2 * g:2 * g + 2, :],
            in_=xres3[:, 2 * g:2 * g + 2, :]),
            inc=("dS", 16), waits=[("cV", w.mark[("Aevac", phid, g)])])
        if seq == 0:
            w.do("sync", lambda e, g=g: e.dma_start(
                out=xres3[:, 2 * g:2 * g + 2, :],
                in_=xin[1, :, 2 * g:2 * g + 2, :]), inc=("dX", 16))
            w.setmark(("xload", 1, g), "dX")



def phase_attn(w, t, v, seq, phid):
    xres3 = t["xres"].rearrange("p (tt d) -> p tt d", d=1024)
    psq, psk, psv, psx = t["psq"], t["psk"], t["psv"], t["psx"]
    xnB = t["xnB"]
    qn, qnT, kn, knT = t["qn"], t["qnT"], t["kn"], t["knT"]
    vb = (t["vb0"], t["vb1"])
    e_all, e_n, eT, afm = t["e_all"], t["e_n"], t["eT"], t["attn_fm"]
    wT = (t["wqT"], t["wkT"], t["wvT"], t["woT"])
    junk = t["junkB"]
    region_guard = v["last_tp"]
    knT_w = knT.rearrange("p (h s q) -> p h s q", h=8, s=2, q=128)
    xnT_w = t["xnTB"].rearrange("p (s b q) -> p s b q", s=2, b=8)

    for i in range(4):
        guard = [("dS", w.mark[("wTout", 3)])]
        if i == 0:
            guard.append(("dT", region_guard))
        w.do("gpsimd", lambda e, i=i: e.dma_start(
            out=wT[i][:], in_=t[f"wt_dram{i}"][:]), inc=("dW", 16),
            waits=guard if i == 0 or i == 1 else [])
        w.setmark(("wT", phid, i), "dW")
    w.do("gpsimd", lambda e: e.dma_start(out=t["Gb"][:], in_=t["G_d"][:]),
         inc=("dW", 16))
    w.setmark(("gb", phid), "dW")
    for gi, gd, rep in ((0, t["gq_d"], t["gq_rep"]), (1, t["gk_d"], t["gk_rep"])):
        w.do("gpsimd", lambda e, gd=gd, rep=rep: e.dma_start(
            out=rep[:], in_=bass.AP(gd.tensor if hasattr(gd, "tensor") else gd,
                                    0, [[0, 128], [1, D]])),
            inc=("dW", 16))
        w.setmark(("repv", phid, gi), "dW")
    w.do("vector", lambda e: e.memset(knT[:], 0.0), inc=("cV", 1),
         waits=[("dT", region_guard)])
    w.do("vector", lambda e: e.memset(vb[0][:], 0.0), inc=("cV", 1))
    w.do("vector", lambda e: e.memset(vb[1][:], 0.0), inc=("cV", 1))
    w.do("vector", lambda e: e.memset(e_n[:], 0.0), inc=("cV", 1))
    w.setmark(("bmemset", phid), "cV")
    if w.engname in ("tensor", "vector", "scalar"):
        w.eng.wait_ge(w.sems["dW"], w.mark[("repv", phid, 1)])
    w.barrier()

    def emit_ln(tt):
        par = tt % 2
        xt = xres3[:, tt, :]
        ldw = []
        if tt <= 1:
            ldw = [("dX", w.mark.get(("xload", seq, 3 if seq == 0 else NG - 1)))]
        w.do("vector", lambda e: e.reduce_sum(
            t["xsumB"][:, par:par + 1], xt, axis=AX.X), inc=("cV", 1), waits=ldw)
        w.setmark(("v1", phid, tt), "cV")
        sq_w = ([("dT", region_guard)] if tt <= 1 else []) + ldw
        w.do("scalar", lambda e: e.activation(
            out=junk[:], in_=xt, func=AF.Square,
            accum_out=t["xsqB"][:, par:par + 1]), inc=("cS", 1), waits=sq_w)
        w.do("scalar", lambda e: e.copy(
            t["xsqBe"][:, par:par + 1], t["xsqB"][:, par:par + 1]),
            inc=("cS", 1), waits=[("cS", w.n["cS"])])
        w.setmark(("s1", phid, tt), "cS")
        w.do("vector", lambda e: e.tensor_scalar_mul(
            t["muB"][:, par:par + 1], t["xsumB"][:, par:par + 1], 1.0 / D),
            inc=("cV", 1), waits=[("cS", w.mark[("s1", phid, tt)]),
                                  ("cV", w.mark[("v1", phid, tt)])])
        w.do("vector", lambda e: e.tensor_scalar(
            out=t["ex2B"][:, par:par + 1], in0=t["xsqBe"][:, par:par + 1],
            scalar1=1.0 / D, scalar2=LNEPS, op0=ALU.mult, op1=ALU.add),
            inc=("cV", 1))
        w.do("vector", lambda e: e.tensor_mul(
            t["mu2B"][:, par:par + 1], t["muB"][:, par:par + 1],
            t["muB"][:, par:par + 1]), inc=("cV", 1), waits=[("cV", w.n["cV"])])
        w.do("vector", lambda e: e.tensor_sub(
            t["varB"][:, par:par + 1], t["ex2B"][:, par:par + 1],
            t["mu2B"][:, par:par + 1]), inc=("cV", 1), waits=[("cV", w.n["cV"])])
        w.setmark(("v1b", phid, tt), "cV")
        w.do("scalar", lambda e: e.activation(
            out=t["lnB"][:, par:par + 1], in_=t["varB"][:, par:par + 1],
            func=AF.Ln), inc=("cS", 1), waits=[("cV", w.mark[("v1b", phid, tt)])])
        w.do("scalar", lambda e: e.activation(
            out=t["rstdB"][:, par:par + 1], in_=t["lnB"][:, par:par + 1],
            func=AF.Exp, scale=-0.5), inc=("cS", 1), waits=[("cS", w.n["cS"])])
        w.setmark(("s2", phid, tt), "cS")
        xw = [("cS", w.mark[("s2", phid, tt)]),
              ("dT", w.mark.get(("xnT", phid, tt - 1)))]
        w.do("vector", lambda e: e.tensor_scalar(
            out=xnB[:], in0=xt, scalar1=t["muB"][:, par:par + 1],
            scalar2=t["rstdB"][:, par:par + 1],
            op0=ALU.subtract, op1=ALU.mult), inc=("cV", 1), waits=xw)
        w.setmark(("v2", phid, tt), "cV")

    def emit_ln_tp(tt):
        par = tt % 2
        tw = [("cV", w.mark[("v2", phid, tt)]),
              ("cP", w.mark.get(("p3", phid, tt - 2)))]
        w.do("sync", lambda e, par=par: e.dma_start(
            out=xnT_w[:, par], in_=xnB[:], transpose=True),
            inc=("dT", 16), waits=tw)
        v["last_tp"] = w.setmark(("xnT", phid, tt), "dT")

    def emit_qk_pe(tt):
        par = tt % 2
        for nm, ps, wTi, ssq, srs in (
                ("q", psq, t["wqT"], t["qssB"], t["rqB"]),
                ("k", psk, t["wkT"], t["kssB"], t["rkB"])):
            isq = nm == "q"
            pw = [("dT", w.mark[("xnT", phid, tt)])]
            if tt == 0:
                pw.append(("dW", w.mark[("wT", phid, 0 if isq else 1)]))
                pw.append(("dW", w.mark[("repv", phid, 1)]))
            else:
                if isq:
                    pw.append(("cV", w.mark[("v6", phid, tt - 1, 0)]))
                    pw.append(("cS", w.mark[("s7", phid, tt - 1)]))
                else:
                    pw.append(("cV", w.mark[("v6", phid, tt - 1, 1)]))
                    pw.append(("cS", w.mark[("s8", phid, tt - 1)]))
            for nh in range(2):
                for k in range(8):
                    w.do("tensor", lambda e, ps=ps, wTi=wTi, nh=nh, k=k: e.matmul(
                        ps[:, nh * 512:(nh + 1) * 512],
                        xnT_w[:, par, k, :],
                        wTi[:, k * 1024 + nh * 512:k * 1024 + nh * 512 + 512],
                        start=(k == 0), stop=(k == 7)),
                        inc=(("cP", 1) if (nh, k) == (1, 7) else None),
                        waits=pw if (nh, k) == (0, 0) else [])
            w.setmark(("p1" if isq else "p2", phid, tt), "cP")
            w.do("scalar", lambda e, ps=ps, ssq=ssq: e.activation(
                out=junk[:], in_=ps[:], func=AF.Square,
                accum_out=ssq[:, par:par + 1]), inc=("cS", 1),
                waits=[("cP", w.mark[("p1" if isq else "p2", phid, tt)])])
            ec = par * 2 + (0 if isq else 1)
            w.do("scalar", lambda e, ssq=ssq, ec=ec: e.copy(
                t["ssqe"][:, ec:ec + 1], ssq[:, par:par + 1]),
                inc=("cS", 1), waits=[("cS", w.n["cS"])])
            w.setmark(("s3" if isq else "s5", phid, tt), "cS")

    def emit_qk_v(tt):
        par = tt % 2
        for nm, ps, srs, grep, dst in (
                ("q", psq, t["rqB"], t["gq_rep"], qn),
                ("k", psk, t["rkB"], t["gk_rep"], kn)):
            isq = nm == "q"
            ec = par * 2 + (0 if isq else 1)
            w.do("vector", lambda e, ec=ec: e.tensor_scalar(
                out=t["msB"][:, par:par + 1], in0=t["ssqe"][:, ec:ec + 1],
                scalar1=1.0 / D, scalar2=RMSEPS, op0=ALU.mult, op1=ALU.add),
                inc=("cV", 1), waits=[("cS", w.mark[("s3" if isq else "s5",
                                                     phid, tt)])])
            w.setmark(("vms", phid, tt, nm), "cV")
            w.do("scalar", lambda e: e.activation(
                out=t["lnB2"][:, par:par + 1], in_=t["msB"][:, par:par + 1],
                func=AF.Ln), inc=("cS", 1),
                waits=[("cV", w.mark[("vms", phid, tt, nm)])])
            w.do("scalar", lambda e, srs=srs: e.activation(
                out=srs[:, par:par + 1], in_=t["lnB2"][:, par:par + 1],
                func=AF.Exp, scale=-0.5), inc=("cS", 1),
                waits=[("cS", w.n["cS"])])
            w.setmark(("s4" if isq else "s6", phid, tt), "cS")
            vw = [("cS", w.mark[("s4" if isq else "s6", phid, tt)]),
                  ("dT", w.mark.get(("qnT" if isq else "knT", phid, tt - 1)))]
            w.do("vector", lambda e, ps=ps, dst=dst, srs=srs, grep=grep:
                 e.scalar_tensor_tensor(
                     out=dst[:], in0=ps[:], scalar=srs[:, par:par + 1],
                     in1=grep[:], op0=ALU.mult, op1=ALU.mult),
                 inc=("cV", 1), waits=vw)
            w.setmark(("v3" if isq else "v4", phid, tt), "cV")
            tw = [("cV", w.mark[("v3" if isq else "v4", phid, tt)]),
                  ("cP", w.mark.get(("p5", phid, tt - 1)))]
            if isq:
                w.do("sync", lambda e: e.dma_start(
                    out=qnT.rearrange("p (b q) -> p b q", b=8), in_=qn[:],
                    transpose=True), inc=("dT", 16), waits=tw)
                v["last_tp"] = w.setmark(("qnT", phid, tt), "dT")
            else:
                w.do("sync", lambda e: e.dma_start(
                    out=knT_w[:, :, par, :], in_=kn[:],
                    transpose=True), inc=("dT", 16), waits=tw)
                v["last_tp"] = w.setmark(("knT", phid, tt), "dT")

    def emit_v(tt):
        par = tt % 2
        pw = [("dT", w.mark[("xnT", phid, tt)])]
        if tt == 0:
            pw.append(("dW", w.mark[("wT", phid, 2)]))
            pw.append(("cV", w.mark[("bmemset", phid)]))
        else:
            pw.append(("cV", w.mark[("v9", phid, tt - 1)]))
        for nh in range(2):
            for k in range(8):
                w.do("tensor", lambda e, nh=nh, k=k: e.matmul(
                    psv[:, nh * 512:(nh + 1) * 512],
                    xnT_w[:, par, k, :],
                    t["wvT"][:, k * 1024 + nh * 512:k * 1024 + nh * 512 + 512],
                    start=(k == 0), stop=(k == 7)),
                    inc=(("cP", 1) if (nh, k) == (1, 7) else None),
                    waits=pw if (nh, k) == (0, 0) else [])
        w.setmark(("p3", phid, tt), "cP")
        w.do("vector", lambda e, par=par: e.tensor_copy(vb[par][:], psv[:]),
             inc=("cV", 1),
             waits=[("cP", w.mark[("p3", phid, tt)]),
                    ("cP", w.mark.get(("p6", phid, tt - 1)))])
        w.setmark(("v5", phid, tt), "cV")

    def emit_scores(tt):
        par = tt % 2
        cur, prev = tt % 2, (tt + 1) % 2
        for rnd, psS in ((0, psq), (1, psk)):
            pw = [("dT", w.mark[("knT", phid, tt)]),
                  ("dT", w.mark[("qnT", phid, tt)])]
            if rnd == 0:
                pw.append(("cV", w.mark[("v3", phid, tt)]))
                pw.append(("cS", w.mark[("s3", phid, tt)]))
            else:
                pw.append(("cV", w.mark[("v4", phid, tt)]))
                pw.append(("cS", w.mark[("s5", phid, tt)]))
            for hh in range(4):
                h = rnd * 4 + hh
                w.do("tensor", lambda e, psS=psS, hh=hh, h=h: e.matmul(
                    psS[:, hh * 256 + 112:hh * 256 + 128],
                    qnT[:, h * 128:(h + 1) * 128],
                    knT_w[:, h, prev, 112:128], start=True, stop=True),
                    waits=pw if hh == 0 else [])
                w.do("tensor", lambda e, psS=psS, hh=hh, h=h: e.matmul(
                    psS[:, hh * 256 + 128:hh * 256 + 256],
                    qnT[:, h * 128:(h + 1) * 128],
                    knT_w[:, h, cur, :], start=True, stop=True),
                    inc=(("cP", 1) if hh == 3 else None))
            w.setmark((("p4", "p5")[rnd], phid, tt), "cP")
            w.do("vector", lambda e, psS=psS, rnd=rnd: e.reduce_max(
                t["negm"][:, par * 8 + rnd * 4:par * 8 + rnd * 4 + 4],
                psS.rearrange("p (hh c) -> p hh c", hh=4)[:, :, 112:256],
                axis=AX.X, negate=True),
                inc=("cV", 1),
                waits=[("cP", w.mark[(("p4", "p5")[rnd], phid, tt)])])
            w.do("vector", lambda e, rnd=rnd: e.tensor_scalar_mul(
                t["negam"][:, par * 8 + rnd * 4:par * 8 + rnd * 4 + 4],
                t["negm"][:, par * 8 + rnd * 4:par * 8 + rnd * 4 + 4], ASC),
                inc=("cV", 1), waits=[("cV", w.n["cV"])])
            w.setmark(("v6", phid, tt, rnd), "cV")
            sw = [("cV", w.mark[("v6", phid, tt, rnd)])]
            if rnd == 0:
                sw.append(("cV", w.mark.get(("v7", phid, tt - 1))))
            for hh in range(4):
                h = rnd * 4 + hh
                w.do("scalar", lambda e, psS=psS, hh=hh, h=h: e.activation(
                    out=e_all[:, h * 144:(h + 1) * 144],
                    in_=psS[:, hh * 256 + 112:hh * 256 + 256],
                    func=AF.Exp, scale=ASC,
                    bias=t["negam"][:, par * 8 + h:par * 8 + h + 1]),
                    inc=(("cS", 1) if hh == 3 else None),
                    waits=sw if hh == 0 else [])
            w.setmark((("s7", "s8")[rnd], phid, tt), "cS")

    def emit_tail(tt):
        par = tt % 2
        cur, prev = tt % 2, (tt + 1) % 2
        xt = xres3[:, tt, :]
        vw = [("cS", w.mark[("s8", phid, tt)]),
              ("dT", w.mark.get(("eT", phid, tt - 1)))]
        if tt == 0:
            vw.append(("dW", w.mark[("gb", phid)]))
        for h in range(8):
            w.do("vector", lambda e, h=h: e.scalar_tensor_tensor(
                out=e_n[:, h * 256 + 112:h * 256 + 256],
                in0=e_all[:, h * 144:(h + 1) * 144], scalar=1.0,
                in1=t["Gb"][:, h * 144:(h + 1) * 144],
                op0=ALU.mult, op1=ALU.mult,
                accum_out=t["rs8"][:, h:h + 1]),
                inc=(("cV", 1) if h == 7 else None),
                waits=vw if h == 0 else [])
        w.setmark(("v7", phid, tt), "cV")
        if tt == 0:
            for h in range(8):
                w.do("vector", lambda e, h=h: e.memset(
                    e_n[:, h * 256 + 112:h * 256 + 128], 0.0), inc=("cV", 1))
            w.do("vector", lambda e: e.reduce_sum(
                t["rs8"][:], e_n.rearrange("p (h c) -> p h c", h=8), axis=AX.X),
                inc=("cV", 1), waits=[("cV", w.n["cV"])])
        w.do("vector", lambda e: e.reciprocal(t["rr8"][:], t["rs8"][:]),
             inc=("cV", 1), waits=[("cV", w.n["cV"])])
        for h in range(8):
            w.do("vector", lambda e, h=h: e.tensor_scalar_mul(
                e_n[:, h * 256 + 112:h * 256 + 256],
                e_n[:, h * 256 + 112:h * 256 + 256],
                t["rr8"][:, h:h + 1]), inc=(("cV", 1) if h == 7 else None),
                waits=[("cV", w.n["cV"])] if h == 0 else [])
        w.setmark(("v8", phid, tt), "cV")
        tw = [("cV", w.mark[("v8", phid, tt)]),
              ("cP", w.mark.get(("p6", phid, tt - 1)))]
        w.do("sync", lambda e: e.dma_start(
            out=eT.rearrange("p (b q) -> p b q", b=16), in_=e_n[:],
            transpose=True), inc=("dT", 16), waits=tw)
        v["last_tp"] = w.setmark(("eT", phid, tt), "dT")
        pw = [("dT", w.mark[("eT", phid, tt)]),
              ("cV", w.mark[("v5", phid, tt)])]
        for h in range(8):
            w.do("tensor", lambda e, h=h: e.matmul(
                psv[:, h * 128:(h + 1) * 128], vb[prev][:, h * 128:(h + 1) * 128],
                eT[:, (2 * h) * 128:(2 * h) * 128 + 128],
                start=True, stop=False), waits=pw if h == 0 else [])
            w.do("tensor", lambda e, h=h: e.matmul(
                psv[:, h * 128:(h + 1) * 128], vb[cur][:, h * 128:(h + 1) * 128],
                eT[:, (2 * h + 1) * 128:(2 * h + 1) * 128 + 128],
                start=False, stop=True),
                inc=(("cP", 1) if h == 7 else None))
        w.setmark(("p6", phid, tt), "cP")
        w.do("vector", lambda e: e.tensor_copy(afm[:], psv[:]),
             inc=("cV", 1),
             waits=[("cP", w.mark[("p6", phid, tt)]),
                    ("cP", w.mark.get(("p7", phid, tt - 1)))])
        w.setmark(("v9", phid, tt), "cV")
        pw = [("cV", w.mark[("v9", phid, tt)]),
              ("cV", w.mark.get(("v10", phid, tt - 1)))]
        if tt == 0:
            pw.append(("dW", w.mark[("wT", phid, 3)]))
        for nh in range(2):
            for h in range(8):
                w.do("tensor", lambda e, nh=nh, h=h: e.matmul(
                    psx[:, nh * 512:(nh + 1) * 512], afm[:, h * 128:(h + 1) * 128],
                    t["woT"][:, h * 1024 + nh * 512:h * 1024 + nh * 512 + 512],
                    start=(h == 0), stop=(h == 7)),
                    inc=(("cP", 1) if (nh, h) == (1, 7) else None),
                    waits=pw if (nh, h) == (0, 0) else [])
        w.setmark(("p7", phid, tt), "cP")
        w.do("vector", lambda e: e.scalar_tensor_tensor(
            out=xt, in0=psx[:], scalar=1.0, in1=xt,
            op0=ALU.mult, op1=ALU.add), inc=("cV", 1),
            waits=[("cP", w.mark[("p7", phid, tt)])])
        w.setmark(("v10", phid, tt), "cV")
        if DEBUG_TAP == (phid, tt):
            stg = t["dbg_stg"]
            for nm, src in (("dbg_qn", qn), ("dbg_kn", kn), ("dbg_vb", vb[cur]),
                            ("dbg_en", e_n), ("dbg_rs", t["rs8"]),
                            ("dbg_afm", afm), ("dbg_xn", xnB),
                            ("dbg_xnT", t["xnTB"][:, 0:1024]),
                            ("dbg_eall", e_all)):
                ncols = src.shape[1]
                w.do("scalar", lambda e, src=src, ncols=ncols: e.copy(
                    stg[:, 0:ncols], src[:]), inc=("cS", 1),
                    waits=[("cV", w.mark[("v10", phid, tt)])])
                mk = w.n["cS"]
                w.do("sync", lambda e, nm=nm, ncols=ncols: e.dma_start(
                    out=t[nm + "_d"][:], in_=stg[:, 0:ncols]), inc=("dS", 16),
                    waits=[("cS", mk)])
                mk2 = w.n["dS"]
                w.do("scalar", lambda e: e.nop(), inc=("cS", 1),
                     waits=[("dS", mk2)])

    ntile = DEBUG_MAX_TILES if DEBUG_MAX_TILES is not None else TPS
    emit_ln(0)
    emit_ln_tp(0)
    if ntile > 1:
        emit_ln(1)
        emit_ln_tp(1)
    emit_qk_pe(0)
    emit_qk_v(0)
    emit_v(0)
    for tt in range(ntile):
        emit_scores(tt)
        if tt + 2 < ntile:
            emit_ln(tt + 2)
        if tt + 1 < ntile:
            emit_qk_pe(tt + 1)
        emit_tail(tt)
        if tt + 2 < ntile:
            emit_ln_tp(tt + 2)
        if tt + 1 < ntile:
            emit_qk_v(tt + 1)
            emit_v(tt + 1)
    if DEBUG_TAPB and phid == 1:
        stg = t["dbg_stg"]
        vm = w.mark[("v10", phid, ntile - 1)]
        w.do("sync", lambda e: e.dma_start(out=t["dbgB_gq_d"][:], in_=t["gq_rep"][:]),
             inc=("dS", 16), waits=[("cV", vm)])
        w.do("sync", lambda e: e.dma_start(out=t["dbgB_gk_d"][:], in_=t["gk_rep"][:]),
             inc=("dS", 16))
        items = [("dbgB_Gb_d", t["Gb"], 0)]
        for i, wn in enumerate(("wq", "wk", "wv", "wo")):
            for c in range(4):
                items.append((f"dbgB_{wn}_d", wT[i][:, c * 2048:(c + 1) * 2048], c))
        for nm, src, c in items:
            ncols = src.shape[1]
            w.do("scalar", lambda e, src=src, ncols=ncols: e.copy(
                stg[:, 0:ncols], src[:]), inc=("cS", 1), waits=[("cV", vm)])
            mk = w.n["cS"]
            w.do("sync", lambda e, nm=nm, c=c, ncols=ncols: e.dma_start(
                out=t[nm][:, c * 2048:c * 2048 + ncols], in_=stg[:, 0:ncols]),
                inc=("dS", 16), waits=[("cS", mk)])
            mk2 = w.n["dS"]
            w.do("scalar", lambda e: e.nop(), inc=("cS", 1), waits=[("dS", mk2)])



def build_nc():
    nc = bass.Bass("TRN2", target_bir_lowering=False)
    t = {}
    t["x_d"] = nc.dram_tensor("x", [2, SEQ, D], F32, kind="ExternalInput")
    t["out_d"] = nc.dram_tensor("out", [2, SEQ, D], F32, kind="ExternalOutput")
    for i in range(4):
        t[f"wt_dram{i}"] = nc.dram_tensor(f"wt_dram{i}", [128, 8 * D], BF16,
                                          kind="Internal")
    t["w1bf_dram"] = nc.dram_tensor("w1bf_dram", [128, 8 * HID], BF16, kind="Internal")
    t["w2bf_dram"] = nc.dram_tensor("w2bf_dram", [128, 32 * D], BF16, kind="Internal")
    for n, shp in (("wq", [D, D]), ("wk", [D, D]), ("wv", [D, D]), ("wo", [D, D]),
                   ("w1", [D, HID]), ("w2", [HID, D]),
                   ("b1r", [32, 128]), ("b2r", [1, D]),
                   ("gq", [1, D]), ("gk", [1, D]), ("G", [128, H * 144]),
                   ("eye", [128, 128])):
        t[n + "_d"] = nc.dram_tensor(n, shp, F32, kind="ExternalInput")
    if DEBUG_TAPB:
        t["dbgB_gq_d"] = nc.dram_tensor("dbgB_gq", [128, D], F32, kind="ExternalOutput")
        t["dbgB_gk_d"] = nc.dram_tensor("dbgB_gk", [128, D], F32, kind="ExternalOutput")
        t["dbgB_Gb_d"] = nc.dram_tensor("dbgB_Gb", [128, H * 144], F32, kind="ExternalOutput")
        for wn in ("wq", "wk", "wv", "wo"):
            t[f"dbgB_{wn}_d"] = nc.dram_tensor(f"dbgB_{wn}", [128, 8 * D], F32,
                                               kind="ExternalOutput")
    if DEBUG_TAP is not None:
        for n, shp in (("dbg_qn", [128, D]), ("dbg_kn", [128, D]),
                       ("dbg_vb", [128, D]), ("dbg_en", [128, 2048]),
                       ("dbg_rs", [128, 8]), ("dbg_afm", [128, D]),
                       ("dbg_xn", [128, D]), ("dbg_xnT", [128, D]),
                       ("dbg_eall", [128, H * 144])):
            t[n + "_d"] = nc.dram_tensor(n, shp, F32, kind="ExternalOutput")

    perm = ExitStack()

    def sb(name, shape, dtype, stk=perm):
        return stk.enter_context(nc.sbuf_tensor(name, shape, dtype))

    t["xres"] = sb("xres", [128, TPS * 1024], F32)
    t["b1f"] = sb("b1f", [128, 32], F32)
    t["b1tmp"] = sb("b1tmp", [32, 128], BF16)
    t["b1T"] = sb("b1T", [128, 32], BF16)
    t["b2row"] = sb("b2row", [1, D], BF16)
    t["onesb"] = sb("onesb", [1, 128], BF16)
    t["idb"] = sb("idb", [128, 128], BF16)
    for n in ("xsumA", "xsqA", "muA", "rstdA", "vtmpA", "vtmp2A", "xsqAe"):
        t[n] = sb(n, [128, TPS], F32)
    for n in ("xsumB", "xsqB", "muB", "ex2B", "mu2B", "varB", "lnB", "rstdB",
              "qssB", "kssB", "rqB", "rkB", "lnB2", "msB", "xsqBe"):
        t[n] = sb(n, [128, 2], F32)
    t["ssqe"] = sb("ssqe", [128, 4], F32)
    t["negm"] = sb("negm", [128, 16], F32)
    t["negam"] = sb("negam", [128, 16], F32)
    t["rs8"] = sb("rs8", [128, 8], F32)
    t["rr8"] = sb("rr8", [128, 8], F32)

    stkA = ExitStack()
    t["w1b"] = sb("w1b", [128, 8 * HID], BF16, stkA)
    t["w2b"] = sb("w2b", [128, 32 * D], BF16, stkA)
    t["xng"] = sb("xng", [128, 2048], BF16, stkA)
    t["xnTg"] = sb("xnTg", [128, 2048], BF16, stkA)
    t["hbuf"] = sb("hbuf", [128, 3 * 256], BF16, stkA)
    stkA.close()

    stkB = ExitStack()
    for n in ("wqT", "wkT", "wvT", "woT", "wstg"):
        t[n] = sb(n, [128, 8 * D], BF16, stkB)
    t["Gb"] = sb("Gb", [128, H * 144], BF16, stkB)
    t["gq_rep"] = sb("gq_rep", [128, D], F32, stkB)
    t["gk_rep"] = sb("gk_rep", [128, D], F32, stkB)
    for n in ("xnB", "qn", "qnT", "kn", "vb0", "vb1", "attn_fm", "junkB"):
        t[n] = sb(n, [128, D], BF16, stkB)
    t["xnTB"] = sb("xnTB", [128, 2 * D], BF16, stkB)
    t["knT"] = sb("knT", [128, 2 * D], BF16, stkB)
    t["e_all"] = sb("e_all", [128, H * 144], BF16, stkB)
    if DEBUG_TAP is not None or DEBUG_TAPB:
        t["dbg_stg"] = sb("dbg_stg", [128, 2048], F32, stkB)
    t["e_n"] = sb("e_n", [128, H * 256], BF16, stkB)
    t["eT"] = sb("eT", [128, H * 256], BF16, stkB)
    stkB.close()

    stkP0 = ExitStack()
    t["pst"] = [stkP0.enter_context(nc.psum_tensor(f"pst{i}", [128, 512], BF16))
                for i in range(2)]
    stkP0.close()
    stkPA = ExitStack()
    t["psh"] = [stkPA.enter_context(nc.psum_tensor(f"psh{i}", [128, 512], F32))
                for i in range(2)]
    t["pso"] = stkPA.enter_context(nc.psum_tensor("pso", [128, 2048], F32))
    stkPA.close()
    stkPB = ExitStack()
    for n in ("psq", "psk", "psv", "psx"):
        t[n] = stkPB.enter_context(nc.psum_tensor(n, [128, 1024], F32))
    stkPB.close()

    with ExitStack() as ctx:
        sems = {}
        for s in ("dW", "dX", "dT", "dS", "cV", "cS", "cP", "cB"):
            sems[s] = ctx.enter_context(nc.semaphore(s))
        block = ctx.enter_context(nc.Block())

        @block.sync
        def _(eng):
            build_walk(Walk("sync", eng, sems), t)

        @block.gpsimd
        def _(eng):
            build_walk(Walk("gpsimd", eng, sems), t)

        @block.tensor
        def _(eng):
            build_walk(Walk("tensor", eng, sems), t)

        @block.vector
        def _(eng):
            build_walk(Walk("vector", eng, sems), t)

        @block.scalar
        def _(eng):
            build_walk(Walk("scalar", eng, sems), t)

    perm.close()
    return nc


_NC_CACHE = None


def _get_nc():
    global _NC_CACHE
    if _NC_CACHE is None:
        _NC_CACHE = build_nc()
    return _NC_CACHE


def _build_G():
    slopes = np.exp2(-np.arange(1, H + 1, dtype=np.float64))
    G = np.zeros((128, H * 144), np.float32)
    i = np.arange(128)[:, None]
    j = np.arange(144)[None, :]
    band = (j >= i) & (j <= i + WIN)
    for h in range(H):
        G[:, h * 144:(h + 1) * 144] = np.where(
            band, np.exp(slopes[h] * (j - WIN - i)), 0.0).astype(np.float32)
    return G


_EYE = np.eye(128, dtype=np.float32)


def kernel(L0_attn_x, L0_attn_wq, L0_attn_wk, L0_attn_wv, L0_attn_wo,
           L0_attn_q_norm_w, L0_attn_k_norm_w,
           L0_ffn_w1, L0_ffn_b1, L0_ffn_w2, L0_ffn_b2):
    x = np.ascontiguousarray(np.asarray(L0_attn_x, np.float32))
    wq = np.ascontiguousarray(np.asarray(L0_attn_wq, np.float32))
    wk = np.ascontiguousarray(np.asarray(L0_attn_wk, np.float32))
    wv = np.ascontiguousarray(np.asarray(L0_attn_wv, np.float32))
    wo = np.ascontiguousarray(np.asarray(L0_attn_wo, np.float32))
    w1 = np.ascontiguousarray(np.asarray(L0_ffn_w1, np.float32))
    w2 = np.ascontiguousarray(np.asarray(L0_ffn_w2, np.float32))
    b1 = np.ascontiguousarray(np.asarray(L0_ffn_b1, np.float32).reshape(32, 128))
    b2 = np.ascontiguousarray(np.asarray(L0_ffn_b2, np.float32).reshape(1, D))
    gq = np.ascontiguousarray(np.asarray(L0_attn_q_norm_w, np.float32).reshape(1, D))
    gk = np.ascontiguousarray(np.asarray(L0_attn_k_norm_w, np.float32).reshape(1, D))
    G = _build_G()

    nc = _get_nc()
    in_maps = []
    for c in range(NCORES):
        in_maps.append(dict(
            x=np.ascontiguousarray(x[2 * c:2 * c + 2]),
            wq=wq, wk=wk, wv=wv, wo=wo, w1=w1, w2=w2,
            b1r=b1, b2r=b2, gq=gq, gk=gk, G=G, eye=_EYE))
    res = run_bass_kernel_spmd(nc, in_maps, core_ids=list(range(NCORES)))
    out = np.concatenate([np.asarray(r["out"]) for r in res.results], axis=0)
    return out.astype(np.float32)


# version 06
# speedup vs baseline: 1.0778x; 1.0778x over previous
import math
import numpy as np
from contextlib import ExitStack

import concourse.bass as bass
import concourse.mybir as mybir
from concourse.bass_utils import run_bass_kernel_spmd

F32 = mybir.dt.float32
BF16 = mybir.dt.bfloat16
AF = mybir.ActivationFunctionType
ALU = mybir.AluOpType
AX = mybir.AxisListType

NCORES = 8
SEQ = 2048
D = 1024
HID = 4096
H = 8
WIN = 16
TPS = 16
NG = 8
ASC = 1.0 / math.sqrt(128.0)
LNEPS = 1e-5
RMSEPS = 1e-6

ENGS = ("sync", "gpsimd", "tensor", "vector", "scalar")

PHASE_MASK = [True] * 6
DEBUG_TAP = None
DEBUG_MAX_TILES = None
DEBUG_TAPB = False


class Walk:

    def __init__(self, engname, eng, sems):
        self.engname = engname
        self.eng = eng
        self.sems = sems
        self.n = {k: 0 for k in sems}
        self.mark = {}
        self.barn = 0

    def do(self, who, fn, inc=None, waits=()):
        if self.engname == who:
            for sem, val in waits:
                if val:
                    self.eng.wait_ge(self.sems[sem], val)
            instr = fn(self.eng)
        else:
            instr = None
        if inc is not None:
            sem, amt = inc
            self.n[sem] += amt
            if instr is not None:
                instr.then_inc(self.sems[sem], amt)

    def setmark(self, key, sem):
        self.mark[key] = self.n[sem]
        return self.mark[key]

    def barrier(self):
        self.barn += 1
        for who in ("gpsimd", "tensor", "vector", "scalar"):
            self.do(who, lambda e: e.nop(), inc=("cB", 1))
        tgt = self.barn * 4
        if self.engname in ENGS:
            self.eng.wait_ge(self.sems["cB"], tgt)


def build_walk(w, t):
    v = dict(
        xng_tp=None,
        xnTg_mm=None,
        pso_ev=None,
        gelu_bank=[None, None],
        w1done=None, w2=[None] * 8,
        last_tp=None,
    )

    w.do("vector", lambda e: e.memset(t["onesb"][:], 1.0), inc=("cV", 1))
    w.setmark("ones", "cV")
    w.do("gpsimd", lambda e: e.dma_start(out=t["b1tmp"][:], in_=t["b1r_d"][:]),
         inc=("dW", 16))
    w.setmark("b1tmp", "dW")
    w.do("gpsimd", lambda e: e.dma_start(out=t["b2row"][:], in_=t["b2r_d"][:]),
         inc=("dW", 16))
    w.setmark("b2row", "dW")
    w.do("sync", lambda e: e.dma_start(out=t["b1T"][:], in_=t["b1tmp"][:],
                                       transpose=True),
         inc=("dT", 16), waits=[("dW", w.mark["b1tmp"])])
    w.setmark("b1T", "dT")
    w.do("scalar", lambda e: e.copy(t["b1f"][:], t["b1T"][:]),
         inc=("cS", 1), waits=[("dT", w.mark["b1T"])])
    xin = t["x_d"].rearrange("s (tt p) d -> s p tt d", p=128)
    xres3 = t["xres"].rearrange("p (tt d) -> p tt d", d=1024)
    for i in range(4):
        w.do("sync",
             lambda e, i=i: e.dma_start(out=xres3[:, i * 4:(i + 1) * 4, :],
                                        in_=xin[0, :, i * 4:(i + 1) * 4, :]),
             inc=("dX", 16))
        w.setmark(("xload", 0, i), "dX")
    w.do("gpsimd", lambda e: e.dma_start(out=t["idb"][:], in_=t["eye_d"][:]),
         inc=("dW", 16))
    w.setmark("idb", "dW")
    wnames = ("wq_d", "wk_d", "wv_d", "wo_d")
    pst = t["pst"]
    for i in range(4):
        wsrc = t[wnames[i]].rearrange("(oc p) d -> p oc d", p=128)
        wstg3 = t["wstg"].rearrange("p (oc d) -> p oc d", oc=8)
        guard = [("cP", w.mark.get(("wtp", i - 1, 15)))]
        w.do("gpsimd", lambda e, wsrc=wsrc, wstg3=wstg3: e.dma_start(
            out=wstg3[:], in_=wsrc[:]), inc=("dW", 16), waits=guard)
        w.setmark(("wstgs", i), "dW")
        for grp in range(16):
            dd, och = grp // 2, (grp % 2) * 4
            tpw = []
            if grp == 0:
                tpw = [("dW", w.mark[("wstgs", i)]), ("dW", w.mark["idb"])]
            cpm = w.mark.get(("wcp", i, grp - 2)) if grp >= 2 else \
                w.mark.get(("wcp", i - 1, 14 + grp))
            if cpm:
                tpw = tpw + [("cV", cpm)]
            for b in range(4):
                oc = och + b
                w.do("tensor", lambda e, dd=dd, oc=oc, b=b, grp=grp: e.transpose(
                    pst[grp % 2][:, b * 128:(b + 1) * 128],
                    t["wstg"][:, oc * 1024 + dd * 128:oc * 1024 + dd * 128 + 128],
                    t["idb"][:]),
                    inc=(("cP", 1) if b == 3 else None),
                    waits=tpw if b == 0 else [])
            w.setmark(("wtp", i, grp), "cP")
            cw = [("cP", w.mark[("wtp", i, grp)])]
            if grp == 0 and i > 0:
                cw.append(("dS", w.mark[("wTout", i - 1)]))
            w.do("vector", lambda e, dd=dd, och=och, grp=grp: e.tensor_copy(
                t["wqT"][:, dd * 1024 + (och // 4) * 512:
                         dd * 1024 + (och // 4) * 512 + 512],
                pst[grp % 2][:]),
                inc=("cV", 1), waits=cw)
            w.setmark(("wcp", i, grp), "cV")
        w.do("sync", lambda e, i=i: e.dma_start(
            out=t[f"wt_dram{i}"][:], in_=t["wqT"][:]),
            inc=("dS", 16), waits=[("cV", w.mark[("wcp", i, 15)])])
        w.setmark(("wTout", i), "dS")

    full = all(PHASE_MASK)
    xres3 = t["xres"].rearrange("p (tt d) -> p tt d", d=1024)
    outv = t["out_d"].rearrange("s (tt p) d -> s p tt d", p=128)
    for seq in range(2):
        if PHASE_MASK[seq * 3 + 0]:
            phase_ffn(w, t, v, seq, phid=seq * 3 + 0,
                      load_weights=(seq == 0) if full else True, store=False)
        if PHASE_MASK[seq * 3 + 1]:
            w.barrier()
            phase_attn(w, t, v, seq, phid=seq * 3 + 1)
            w.barrier()
        if PHASE_MASK[seq * 3 + 2]:
            phase_ffn(w, t, v, seq, phid=seq * 3 + 2,
                      load_weights=True, store=True)
        else:
            cv_now = w.n["cV"]
            w.do("sync", lambda e, seq=seq: e.dma_start(
                out=outv[seq], in_=xres3[:]),
                inc=("dS", 16), waits=[("cV", cv_now)])
            if seq == 0:
                w.do("sync", lambda e: e.dma_start(out=xres3[:], in_=xin[1]),
                     inc=("dX", 16))
                for g in range(NG):
                    w.setmark(("xload", 1, g), "dX")

    total_dS = w.n["dS"]
    if w.engname == "sync":
        w.eng.wait_ge(w.sems["dS"], total_dS)



def phase_ffn(w, t, v, seq, phid, load_weights, store):
    xres = t["xres"]
    xres3 = xres.rearrange("p (tt d) -> p tt d", d=1024)
    xng, xnTg, hbuf = t["xng"], t["xnTg"], t["hbuf"]
    w1b, w2b = t["w1b"], t["w2b"]
    psh, pso = t["psh"], t["pso"]
    xin = t["x_d"].rearrange("s (tt p) d -> s p tt d", p=128)
    outv = t["out_d"].rearrange("s (tt p) d -> s p tt d", p=128)

    if load_weights and phid == 0:
        guard = [("dS", w.mark[("wTout", 3)])]
        for k in range(8):
            w.do("gpsimd",
                 lambda e, k=k: e.dma_start(out=w1b[:, k * 4096:(k + 1) * 4096],
                                            in_=t["w1_d"][k * 128:(k + 1) * 128, :]),
                 inc=("dW", 16), waits=guard if k == 0 else [])
        v["w1done"] = w.setmark(("w1", phid), "dW")
        w2v = t["w2_d"].rearrange("(j c p) o -> j p c o", j=8, p=128)
        w2b4 = w2b.rearrange("p (j c o) -> p j c o", j=8, c=4)
        for j in range(8):
            w.do("gpsimd",
                 lambda e, j=j: e.dma_start(out=w2b4[:, j], in_=w2v[j]),
                 inc=("dW", 16))
            v["w2"][j] = w.setmark(("w2", phid, j), "dW")
    elif load_weights:
        guard = [("dT", v["last_tp"]), ("dS", w.mark["wbfout"])]
        w1v = w1b.rearrange("p (k m) -> p k m", k=8)
        w1dv = t["w1bf_dram"].rearrange("p (k m) -> p k m", k=8)
        for j in range(4):
            w.do("gpsimd", lambda e, j=j: e.dma_start(
                out=w1v[:, :, j * 1024:(j + 1) * 1024],
                in_=w1dv[:, :, j * 1024:(j + 1) * 1024]),
                inc=("dW", 16), waits=guard if j == 0 else [])
            w.setmark(("w1s", phid, j), "dW")
        v["w1done"] = w.mark[("w1s", phid, 0)]
        v["w1slice"] = [w.mark[("w1s", phid, j)] for j in range(4)]
        for j in range(4):
            w.do("gpsimd", lambda e, j=j: e.dma_start(
                out=w2b[:, j * 8192:(j + 1) * 8192],
                in_=t["w2bf_dram"][:, j * 8192:(j + 1) * 8192]),
                inc=("dW", 16))
            w.setmark(("w2s", phid, j), "dW")
            for jj in range(2):
                v["w2"][j * 2 + jj] = w.mark[("w2s", phid, j)]

    for tt in range(TPS):
        xt = xres3[:, tt, :]
        wv, ws = [], []
        if phid == 0:
            wv = [("dX", w.mark[("xload", 0, tt // 4)])]
            ws = list(wv)
        elif phid == 3:
            wv = [("dX", w.mark[("xload", 1, tt // 2)])]
            ws = list(wv)
        if tt == 0:
            ws = ws + [("dT", v["xng_tp"])]
        w.do("vector", lambda e, xt=xt, tt=tt: e.reduce_sum(
            t["xsumA"][:, tt:tt + 1], xt, axis=AX.X), inc=("cV", 1), waits=wv)
        w.do("scalar", lambda e, xt=xt, tt=tt: e.activation(
            out=xng[:, 0:1024], in_=xt, func=AF.Square,
            accum_out=t["xsqA"][:, tt:tt + 1]), inc=("cS", 1), waits=ws)
    w.do("scalar", lambda e: e.copy(t["xsqAe"][:], t["xsqA"][:]),
         inc=("cS", 1), waits=[("cS", w.n["cS"])])
    w.setmark(("a1s", phid), "cS")
    w.do("vector", lambda e: e.tensor_scalar_mul(t["muA"][:], t["xsumA"][:], 1.0 / D),
         inc=("cV", 1), waits=[("cS", w.mark[("a1s", phid)]), ("cV", w.n["cV"])])
    w.do("vector", lambda e: e.tensor_scalar(
        out=t["vtmpA"][:], in0=t["xsqAe"][:], scalar1=1.0 / D, scalar2=LNEPS,
        op0=ALU.mult, op1=ALU.add), inc=("cV", 1))
    w.do("vector", lambda e: e.tensor_mul(t["vtmp2A"][:], t["muA"][:], t["muA"][:]),
         inc=("cV", 1), waits=[("cV", w.n["cV"])])
    w.do("vector", lambda e: e.tensor_sub(t["vtmpA"][:], t["vtmpA"][:], t["vtmp2A"][:]),
         inc=("cV", 1), waits=[("cV", w.n["cV"])])
    w.setmark(("varA", phid), "cV")
    w.do("scalar", lambda e: e.activation(out=t["vtmp2A"][:], in_=t["vtmpA"][:],
                                          func=AF.Ln),
         inc=("cS", 1), waits=[("cV", w.mark[("varA", phid)])])
    w.do("scalar", lambda e: e.activation(out=t["rstdA"][:], in_=t["vtmp2A"][:],
                                          func=AF.Exp, scale=-0.5), inc=("cS", 1),
         waits=[("cS", w.n["cS"])])
    w.setmark(("rstdA", phid), "cS")

    xn_w = xng.rearrange("p (k two f) -> p k two f", k=8, two=2, f=128)
    for g in range(NG):
        for sub in range(2):
            tt = 2 * g + sub
            waits = []
            if sub == 0:
                waits.append(("cS", w.mark[("rstdA", phid)]))
                waits.append(("dT", v["xng_tp"]))
                waits.append(("cV", w.n["cV"]))
            w.do("vector", lambda e, tt=tt, sub=sub: e.tensor_scalar(
                out=xn_w[:, :, sub, :], in0=xres3[:, tt, :],
                scalar1=t["muA"][:, tt:tt + 1], scalar2=t["rstdA"][:, tt:tt + 1],
                op0=ALU.subtract, op1=ALU.mult), inc=("cV", 1), waits=waits)
        w.setmark(("Axn", phid, g), "cV")
        tw = [("cV", w.mark[("Axn", phid, g)]), ("cP", v["xnTg_mm"])]
        w.do("sync", lambda e: e.dma_start(
            out=xnTg.rearrange("p (b q) -> p b q", b=16),
            in_=xng[:], transpose=True), inc=("dT", 16), waits=tw)
        v["xng_tp"] = w.setmark(("AxnT", phid, g), "dT")
        v["last_tp"] = v["xng_tp"]
        if g > 0:
            emit_ffn_evac(w, t, v, seq, phid, g - 1, store, outv, xres3, xin)

        for m in range(32):
            pw = [("cS", v["gelu_bank"][m % 2])]
            if m == 0:
                pw.append(("dT", w.mark[("AxnT", phid, g)]))
            if g == 0:
                sl = v.get("w1slice")
                if sl and m % 8 == 0:
                    pw.append(("dW", sl[m // 8]))
                elif m == 0:
                    pw.append(("dW", v["w1done"]))
            bank = psh[m % 2]
            for k in range(8):
                w.do("tensor", lambda e, k=k, m=m, bank=bank: e.matmul(
                    bank[:, 0:256],
                    w1b[:, k * 4096 + m * 128:k * 4096 + (m + 1) * 128],
                    xnTg[:, k * 256:(k + 1) * 256],
                    start=(k == 0), stop=(k == 7)),
                    inc=(("cP", 1) if k == 7 else None),
                    waits=pw if k == 0 else [])
            w.setmark(("Ah", phid, g, m), "cP")
            w.do("scalar", lambda e, m=m, bank=bank: e.activation(
                out=hbuf[:, (m % 3) * 256:(m % 3 + 1) * 256], in_=bank[:, 0:256],
                func=AF.Gelu, bias=t["b1f"][:, m:m + 1]),
                inc=("cS", 1), waits=[("cP", w.mark[("Ah", phid, g, m)])])
            v["gelu_bank"][m % 2] = w.setmark(("Agelu", phid, g, m), "cS")
            if m >= 1:
                emit_mm2(w, t, v, phid, g, m - 1)
        emit_mm2(w, t, v, phid, g, 31)
        v["xnTg_mm"] = w.mark[("Ah", phid, g, 31)]
        for sub in range(2):
            for nh in range(2):
                w.do("tensor", lambda e, sub=sub, nh=nh: e.matmul(
                    pso[:, sub * 1024 + nh * 512:sub * 1024 + nh * 512 + 512],
                    t["onesb"][:], t["b2row"][:, nh * 512:nh * 512 + 512],
                    start=False, stop=True),
                    inc=(("cP", 1) if (sub, nh) == (1, 1) else None),
                    waits=[("dW", w.mark["b2row"]), ("cV", w.mark["ones"])]
                    if (sub, nh) == (0, 0) else [])
        w.setmark(("Aout", phid, g), "cP")
    emit_ffn_evac(w, t, v, seq, phid, NG - 1, store, outv, xres3, xin)
    if load_weights and phid == 0:
        w.do("sync", lambda e: e.dma_start(out=t["w1bf_dram"][:], in_=w1b[:]),
             inc=("dS", 16), waits=[("dW", v["w1done"])])
        w.do("sync", lambda e: e.dma_start(out=t["w2bf_dram"][:], in_=w2b[:]),
             inc=("dS", 16), waits=[("dW", v["w2"][7])])
        w.setmark("wbfout", "dS")


def emit_mm2(w, t, v, phid, g, m):
    pso, hbuf, w2b = t["pso"], t["hbuf"], t["w2b"]
    pw = [("cS", w.mark[("Agelu", phid, g, m)])]
    if m == 0:
        pw.append(("cV", v["pso_ev"]))
    if g == 0 and v["w2"][m // 4]:
        pw.append(("dW", v["w2"][m // 4]))
    for sub in range(2):
        for nh in range(2):
            w.do("tensor", lambda e, sub=sub, nh=nh, m=m: e.matmul(
                pso[:, sub * 1024 + nh * 512:sub * 1024 + nh * 512 + 512],
                hbuf[:, (m % 3) * 256 + sub * 128:(m % 3) * 256 + sub * 128 + 128],
                w2b[:, m * 1024 + nh * 512:m * 1024 + nh * 512 + 512],
                start=(m == 0), stop=False),
                waits=pw if (sub, nh) == (0, 0) else [])


def emit_ffn_evac(w, t, v, seq, phid, g, store, outv, xres3, xin):
    pso = t["pso"]
    for sub in range(2):
        tt = 2 * g + sub
        w.do("vector", lambda e, tt=tt, sub=sub: e.scalar_tensor_tensor(
            out=xres3[:, tt, :], in0=pso[:, sub * 1024:(sub + 1) * 1024],
            scalar=0.5, in1=xres3[:, tt, :],
            op0=ALU.mult, op1=ALU.add),
            inc=("cV", 1),
            waits=[("cP", w.mark[("Aout", phid, g)])] if sub == 0 else [])
    v["pso_ev"] = w.setmark(("Aevac", phid, g), "cV")
    if store:
        w.do("sync", lambda e, g=g: e.dma_start(
            out=outv[seq, :, 2 * g:2 * g + 2, :],
            in_=xres3[:, 2 * g:2 * g + 2, :]),
            inc=("dS", 16), waits=[("cV", w.mark[("Aevac", phid, g)])])
        if seq == 0:
            w.do("sync", lambda e, g=g: e.dma_start(
                out=xres3[:, 2 * g:2 * g + 2, :],
                in_=xin[1, :, 2 * g:2 * g + 2, :]), inc=("dX", 16))
            w.setmark(("xload", 1, g), "dX")



def phase_attn(w, t, v, seq, phid):
    xres3 = t["xres"].rearrange("p (tt d) -> p tt d", d=1024)
    psq, psk, psv, psx = t["psq"], t["psk"], t["psv"], t["psx"]
    xnB = t["xnB"]
    qn, qnT, kn, knT = t["qn"], t["qnT"], t["kn"], t["knT"]
    vb = (t["vb0"], t["vb1"])
    e_all, e_n, eT, afm = t["e_all"], t["e_n"], t["eT"], t["attn_fm"]
    wT = (t["wqT"], t["wkT"], t["wvT"], t["woT"])
    junk = t["junkB"]
    region_guard = v["last_tp"]
    knT_w = knT.rearrange("p (h s q) -> p h s q", h=8, s=2, q=128)
    xnT_w = t["xnTB"].rearrange("p (s b q) -> p s b q", s=2, b=8)

    for i in range(4):
        guard = [("dS", w.mark[("wTout", 3)])]
        if i == 0:
            guard.append(("dT", region_guard))
        w.do("gpsimd", lambda e, i=i: e.dma_start(
            out=wT[i][:], in_=t[f"wt_dram{i}"][:]), inc=("dW", 16),
            waits=guard if i == 0 or i == 1 else [])
        w.setmark(("wT", phid, i), "dW")
    w.do("gpsimd", lambda e: e.dma_start(out=t["Gb"][:], in_=t["G_d"][:]),
         inc=("dW", 16))
    w.setmark(("gb", phid), "dW")
    for gi, gd, rep in ((0, t["gq_d"], t["gq_rep"]), (1, t["gk_d"], t["gk_rep"])):
        w.do("gpsimd", lambda e, gd=gd, rep=rep: e.dma_start(
            out=rep[:], in_=bass.AP(gd.tensor if hasattr(gd, "tensor") else gd,
                                    0, [[0, 128], [1, D]])),
            inc=("dW", 16))
        w.setmark(("repv", phid, gi), "dW")
    w.do("vector", lambda e: e.memset(knT[:], 0.0), inc=("cV", 1),
         waits=[("dT", region_guard)])
    w.do("vector", lambda e: e.memset(vb[0][:], 0.0), inc=("cV", 1))
    w.do("vector", lambda e: e.memset(vb[1][:], 0.0), inc=("cV", 1))
    w.do("vector", lambda e: e.memset(e_n[:], 0.0), inc=("cV", 1))
    w.setmark(("bmemset", phid), "cV")
    if w.engname in ("tensor", "vector", "scalar"):
        w.eng.wait_ge(w.sems["dW"], w.mark[("repv", phid, 1)])
    w.barrier()

    def emit_ln(tt):
        par = tt % 2
        xt = xres3[:, tt, :]
        ldw = []
        if tt <= 1:
            ldw = [("dX", w.mark.get(("xload", seq, 3 if seq == 0 else NG - 1)))]
        w.do("vector", lambda e: e.reduce_sum(
            t["xsumB"][:, par:par + 1], xt, axis=AX.X), inc=("cV", 1), waits=ldw)
        w.setmark(("v1", phid, tt), "cV")
        sq_w = ([("dT", region_guard)] if tt <= 1 else []) + ldw
        w.do("scalar", lambda e: e.activation(
            out=junk[:], in_=xt, func=AF.Square,
            accum_out=t["xsqB"][:, par:par + 1]), inc=("cS", 1), waits=sq_w)
        w.do("scalar", lambda e: e.copy(
            t["xsqBe"][:, par:par + 1], t["xsqB"][:, par:par + 1]),
            inc=("cS", 1), waits=[("cS", w.n["cS"])])
        w.setmark(("s1", phid, tt), "cS")
        w.do("vector", lambda e: e.tensor_scalar_mul(
            t["muB"][:, par:par + 1], t["xsumB"][:, par:par + 1], 1.0 / D),
            inc=("cV", 1), waits=[("cS", w.mark[("s1", phid, tt)]),
                                  ("cV", w.mark[("v1", phid, tt)])])
        w.do("vector", lambda e: e.tensor_scalar(
            out=t["ex2B"][:, par:par + 1], in0=t["xsqBe"][:, par:par + 1],
            scalar1=1.0 / D, scalar2=LNEPS, op0=ALU.mult, op1=ALU.add),
            inc=("cV", 1))
        w.do("vector", lambda e: e.tensor_mul(
            t["mu2B"][:, par:par + 1], t["muB"][:, par:par + 1],
            t["muB"][:, par:par + 1]), inc=("cV", 1), waits=[("cV", w.n["cV"])])
        w.do("vector", lambda e: e.tensor_sub(
            t["varB"][:, par:par + 1], t["ex2B"][:, par:par + 1],
            t["mu2B"][:, par:par + 1]), inc=("cV", 1), waits=[("cV", w.n["cV"])])
        w.setmark(("v1b", phid, tt), "cV")
        w.do("scalar", lambda e: e.activation(
            out=t["lnB"][:, par:par + 1], in_=t["varB"][:, par:par + 1],
            func=AF.Ln), inc=("cS", 1), waits=[("cV", w.mark[("v1b", phid, tt)])])
        w.do("scalar", lambda e: e.activation(
            out=t["rstdB"][:, par:par + 1], in_=t["lnB"][:, par:par + 1],
            func=AF.Exp, scale=-0.5), inc=("cS", 1), waits=[("cS", w.n["cS"])])
        w.setmark(("s2", phid, tt), "cS")
        xw = [("cS", w.mark[("s2", phid, tt)]),
              ("dT", w.mark.get(("xnT", phid, tt - 1)))]
        w.do("vector", lambda e: e.tensor_scalar(
            out=xnB[:], in0=xt, scalar1=t["muB"][:, par:par + 1],
            scalar2=t["rstdB"][:, par:par + 1],
            op0=ALU.subtract, op1=ALU.mult), inc=("cV", 1), waits=xw)
        w.setmark(("v2", phid, tt), "cV")
        tw = [("cV", w.mark[("v2", phid, tt)]),
              ("cP", w.mark.get(("p3", phid, tt - 2)))]
        w.do("sync", lambda e: e.dma_start(
            out=xnT_w[:, par], in_=xnB[:], transpose=True),
            inc=("dT", 16), waits=tw)
        v["last_tp"] = w.setmark(("xnT", phid, tt), "dT")

    def emit_qk_pe(tt):
        par = tt % 2
        for nm, ps, wTi, ssq, srs in (
                ("q", psq, t["wqT"], t["qssB"], t["rqB"]),
                ("k", psk, t["wkT"], t["kssB"], t["rkB"])):
            isq = nm == "q"
            pw = [("dT", w.mark[("xnT", phid, tt)])]
            if tt == 0:
                pw.append(("dW", w.mark[("wT", phid, 0 if isq else 1)]))
                pw.append(("dW", w.mark[("repv", phid, 1)]))
            else:
                if isq:
                    pw.append(("cV", w.mark[("v6", phid, tt - 1, 0)]))
                    pw.append(("cS", w.mark[("s7", phid, tt - 1)]))
                else:
                    pw.append(("cV", w.mark[("v6", phid, tt - 1, 1)]))
                    pw.append(("cS", w.mark[("s8", phid, tt - 1)]))
            for nh in range(2):
                for k in range(8):
                    w.do("tensor", lambda e, ps=ps, wTi=wTi, nh=nh, k=k: e.matmul(
                        ps[:, nh * 512:(nh + 1) * 512],
                        xnT_w[:, par, k, :],
                        wTi[:, k * 1024 + nh * 512:k * 1024 + nh * 512 + 512],
                        start=(k == 0), stop=(k == 7)),
                        inc=(("cP", 1) if (nh, k) == (1, 7) else None),
                        waits=pw if (nh, k) == (0, 0) else [])
            w.setmark(("p1" if isq else "p2", phid, tt), "cP")
            w.do("scalar", lambda e, ps=ps, ssq=ssq: e.activation(
                out=junk[:], in_=ps[:], func=AF.Square,
                accum_out=ssq[:, par:par + 1]), inc=("cS", 1),
                waits=[("cP", w.mark[("p1" if isq else "p2", phid, tt)])])
            ec = par * 2 + (0 if isq else 1)
            w.do("scalar", lambda e, ssq=ssq, ec=ec: e.copy(
                t["ssqe"][:, ec:ec + 1], ssq[:, par:par + 1]),
                inc=("cS", 1), waits=[("cS", w.n["cS"])])
            w.setmark(("s3" if isq else "s5", phid, tt), "cS")

    def emit_qk_v(tt, part):
        par = tt % 2
        for nm, ps, srs, grep, dst in (
                ("q", psq, t["rqB"], t["gq_rep"], qn),
                ("k", psk, t["rkB"], t["gk_rep"], kn)):
            isq = nm == "q"
            if part == 1:
                emit_qk_tp(tt, isq)
                continue
            ec = par * 2 + (0 if isq else 1)
            w.do("vector", lambda e, ec=ec: e.tensor_scalar(
                out=t["msB"][:, par:par + 1], in0=t["ssqe"][:, ec:ec + 1],
                scalar1=1.0 / D, scalar2=RMSEPS, op0=ALU.mult, op1=ALU.add),
                inc=("cV", 1), waits=[("cS", w.mark[("s3" if isq else "s5",
                                                     phid, tt)])])
            w.setmark(("vms", phid, tt, nm), "cV")
            w.do("scalar", lambda e: e.activation(
                out=t["lnB2"][:, par:par + 1], in_=t["msB"][:, par:par + 1],
                func=AF.Ln), inc=("cS", 1),
                waits=[("cV", w.mark[("vms", phid, tt, nm)])])
            w.do("scalar", lambda e, srs=srs: e.activation(
                out=srs[:, par:par + 1], in_=t["lnB2"][:, par:par + 1],
                func=AF.Exp, scale=-0.5), inc=("cS", 1),
                waits=[("cS", w.n["cS"])])
            w.setmark(("s4" if isq else "s6", phid, tt), "cS")
            vw = [("cS", w.mark[("s4" if isq else "s6", phid, tt)]),
                  ("dT", w.mark.get(("qnT" if isq else "knT", phid, tt - 1)))]
            w.do("vector", lambda e, ps=ps, dst=dst, srs=srs, grep=grep:
                 e.scalar_tensor_tensor(
                     out=dst[:], in0=ps[:], scalar=srs[:, par:par + 1],
                     in1=grep[:], op0=ALU.mult, op1=ALU.mult),
                 inc=("cV", 1), waits=vw)
            w.setmark(("v3" if isq else "v4", phid, tt), "cV")

    def emit_qk_tp(tt, isq):
        par = tt % 2
        tw = [("cV", w.mark[("v3" if isq else "v4", phid, tt)]),
              ("cP", w.mark.get(("p5", phid, tt - 1)))]
        if isq:
            w.do("sync", lambda e: e.dma_start(
                out=qnT.rearrange("p (b q) -> p b q", b=8), in_=qn[:],
                transpose=True), inc=("dT", 16), waits=tw)
            v["last_tp"] = w.setmark(("qnT", phid, tt), "dT")
        else:
            w.do("sync", lambda e: e.dma_start(
                out=knT_w[:, :, par, :], in_=kn[:],
                transpose=True), inc=("dT", 16), waits=tw)
            v["last_tp"] = w.setmark(("knT", phid, tt), "dT")

    def emit_v(tt):
        par = tt % 2
        pw = [("dT", w.mark[("xnT", phid, tt)])]
        if tt == 0:
            pw.append(("dW", w.mark[("wT", phid, 2)]))
            pw.append(("cV", w.mark[("bmemset", phid)]))
        else:
            pw.append(("cV", w.mark[("v9", phid, tt - 1)]))
        for nh in range(2):
            for k in range(8):
                w.do("tensor", lambda e, nh=nh, k=k: e.matmul(
                    psv[:, nh * 512:(nh + 1) * 512],
                    xnT_w[:, par, k, :],
                    t["wvT"][:, k * 1024 + nh * 512:k * 1024 + nh * 512 + 512],
                    start=(k == 0), stop=(k == 7)),
                    inc=(("cP", 1) if (nh, k) == (1, 7) else None),
                    waits=pw if (nh, k) == (0, 0) else [])
        w.setmark(("p3", phid, tt), "cP")
        w.do("vector", lambda e, par=par: e.tensor_copy(vb[par][:], psv[:]),
             inc=("cV", 1),
             waits=[("cP", w.mark[("p3", phid, tt)]),
                    ("cP", w.mark.get(("p6", phid, tt - 1)))])
        w.setmark(("v5", phid, tt), "cV")

    def emit_scores(tt):
        par = tt % 2
        cur, prev = tt % 2, (tt + 1) % 2
        for rnd, psS in ((0, psq), (1, psk)):
            pw = [("dT", w.mark[("knT", phid, tt)]),
                  ("dT", w.mark[("qnT", phid, tt)])]
            if rnd == 0:
                pw.append(("cV", w.mark[("v3", phid, tt)]))
                pw.append(("cS", w.mark[("s3", phid, tt)]))
            else:
                pw.append(("cV", w.mark[("v4", phid, tt)]))
                pw.append(("cS", w.mark[("s5", phid, tt)]))
            for hh in range(4):
                h = rnd * 4 + hh
                w.do("tensor", lambda e, psS=psS, hh=hh, h=h: e.matmul(
                    psS[:, hh * 256 + 112:hh * 256 + 128],
                    qnT[:, h * 128:(h + 1) * 128],
                    knT_w[:, h, prev, 112:128], start=True, stop=True),
                    waits=pw if hh == 0 else [])
                w.do("tensor", lambda e, psS=psS, hh=hh, h=h: e.matmul(
                    psS[:, hh * 256 + 128:hh * 256 + 256],
                    qnT[:, h * 128:(h + 1) * 128],
                    knT_w[:, h, cur, :], start=True, stop=True),
                    inc=(("cP", 1) if hh == 3 else None))
            w.setmark((("p4", "p5")[rnd], phid, tt), "cP")
            w.do("vector", lambda e, psS=psS, rnd=rnd: e.reduce_max(
                t["negm"][:, par * 8 + rnd * 4:par * 8 + rnd * 4 + 4],
                psS.rearrange("p (hh c) -> p hh c", hh=4)[:, :, 112:256],
                axis=AX.X, negate=True),
                inc=("cV", 1),
                waits=[("cP", w.mark[(("p4", "p5")[rnd], phid, tt)])])
            w.do("vector", lambda e, rnd=rnd: e.tensor_scalar_mul(
                t["negam"][:, par * 8 + rnd * 4:par * 8 + rnd * 4 + 4],
                t["negm"][:, par * 8 + rnd * 4:par * 8 + rnd * 4 + 4], ASC),
                inc=("cV", 1), waits=[("cV", w.n["cV"])])
            w.setmark(("v6", phid, tt, rnd), "cV")
            sw = [("cV", w.mark[("v6", phid, tt, rnd)])]
            if rnd == 0:
                sw.append(("cV", w.mark.get(("v7", phid, tt - 1))))
            for hh in range(4):
                h = rnd * 4 + hh
                w.do("scalar", lambda e, psS=psS, hh=hh, h=h: e.activation(
                    out=e_all[:, h * 144:(h + 1) * 144],
                    in_=psS[:, hh * 256 + 112:hh * 256 + 256],
                    func=AF.Exp, scale=ASC,
                    bias=t["negam"][:, par * 8 + h:par * 8 + h + 1]),
                    inc=(("cS", 1) if hh == 3 else None),
                    waits=sw if hh == 0 else [])
            w.setmark((("s7", "s8")[rnd], phid, tt), "cS")

    def emit_tail(tt):
        par = tt % 2
        cur, prev = tt % 2, (tt + 1) % 2
        xt = xres3[:, tt, :]
        vw = [("cS", w.mark[("s8", phid, tt)]),
              ("dT", w.mark.get(("eT", phid, tt - 1)))]
        if tt == 0:
            vw.append(("dW", w.mark[("gb", phid)]))
        for h in range(8):
            w.do("vector", lambda e, h=h: e.scalar_tensor_tensor(
                out=e_n[:, h * 256 + 112:h * 256 + 256],
                in0=e_all[:, h * 144:(h + 1) * 144], scalar=1.0,
                in1=t["Gb"][:, h * 144:(h + 1) * 144],
                op0=ALU.mult, op1=ALU.mult,
                accum_out=t["rs8"][:, h:h + 1]),
                inc=(("cV", 1) if h == 7 else None),
                waits=vw if h == 0 else [])
        w.setmark(("v7", phid, tt), "cV")
        if tt == 0:
            for h in range(8):
                w.do("vector", lambda e, h=h: e.memset(
                    e_n[:, h * 256 + 112:h * 256 + 128], 0.0), inc=("cV", 1))
            w.do("vector", lambda e: e.reduce_sum(
                t["rs8"][:], e_n.rearrange("p (h c) -> p h c", h=8), axis=AX.X),
                inc=("cV", 1), waits=[("cV", w.n["cV"])])
        w.do("vector", lambda e: e.reciprocal(t["rr8"][:], t["rs8"][:]),
             inc=("cV", 1), waits=[("cV", w.n["cV"])])
        for h in range(8):
            w.do("vector", lambda e, h=h: e.tensor_scalar_mul(
                e_n[:, h * 256 + 112:h * 256 + 256],
                e_n[:, h * 256 + 112:h * 256 + 256],
                t["rr8"][:, h:h + 1]), inc=(("cV", 1) if h == 7 else None),
                waits=[("cV", w.n["cV"])] if h == 0 else [])
        w.setmark(("v8", phid, tt), "cV")
        tw = [("cV", w.mark[("v8", phid, tt)]),
              ("cP", w.mark.get(("p6", phid, tt - 1)))]
        w.do("sync", lambda e: e.dma_start(
            out=eT.rearrange("p (b q) -> p b q", b=16), in_=e_n[:],
            transpose=True), inc=("dT", 16), waits=tw)
        v["last_tp"] = w.setmark(("eT", phid, tt), "dT")
        pw = [("dT", w.mark[("eT", phid, tt)]),
              ("cV", w.mark[("v5", phid, tt)])]
        for h in range(8):
            w.do("tensor", lambda e, h=h: e.matmul(
                psv[:, h * 128:(h + 1) * 128], vb[prev][:, h * 128:(h + 1) * 128],
                eT[:, (2 * h) * 128:(2 * h) * 128 + 128],
                start=True, stop=False), waits=pw if h == 0 else [])
            w.do("tensor", lambda e, h=h: e.matmul(
                psv[:, h * 128:(h + 1) * 128], vb[cur][:, h * 128:(h + 1) * 128],
                eT[:, (2 * h + 1) * 128:(2 * h + 1) * 128 + 128],
                start=False, stop=True),
                inc=(("cP", 1) if h == 7 else None))
        w.setmark(("p6", phid, tt), "cP")

    def emit_tail_b(tt):
        par = tt % 2
        cur, prev = tt % 2, (tt + 1) % 2
        xt = xres3[:, tt, :]
        w.do("vector", lambda e: e.tensor_copy(afm[:], psv[:]),
             inc=("cV", 1),
             waits=[("cP", w.mark[("p6", phid, tt)]),
                    ("cP", w.mark.get(("p7", phid, tt - 1)))])
        w.setmark(("v9", phid, tt), "cV")
        pw = [("cV", w.mark[("v9", phid, tt)]),
              ("cV", w.mark.get(("v10", phid, tt - 1)))]
        if tt == 0:
            pw.append(("dW", w.mark[("wT", phid, 3)]))
        for nh in range(2):
            for h in range(8):
                w.do("tensor", lambda e, nh=nh, h=h: e.matmul(
                    psx[:, nh * 512:(nh + 1) * 512], afm[:, h * 128:(h + 1) * 128],
                    t["woT"][:, h * 1024 + nh * 512:h * 1024 + nh * 512 + 512],
                    start=(h == 0), stop=(h == 7)),
                    inc=(("cP", 1) if (nh, h) == (1, 7) else None),
                    waits=pw if (nh, h) == (0, 0) else [])
        w.setmark(("p7", phid, tt), "cP")
        w.do("vector", lambda e: e.scalar_tensor_tensor(
            out=xt, in0=psx[:], scalar=1.0, in1=xt,
            op0=ALU.mult, op1=ALU.add), inc=("cV", 1),
            waits=[("cP", w.mark[("p7", phid, tt)])])
        w.setmark(("v10", phid, tt), "cV")
        if DEBUG_TAP == (phid, tt):
            stg = t["dbg_stg"]
            for nm, src in (("dbg_qn", qn), ("dbg_kn", kn), ("dbg_vb", vb[cur]),
                            ("dbg_en", e_n), ("dbg_rs", t["rs8"]),
                            ("dbg_afm", afm), ("dbg_xn", xnB),
                            ("dbg_xnT", t["xnTB"][:, 0:1024]),
                            ("dbg_eall", e_all)):
                ncols = src.shape[1]
                w.do("scalar", lambda e, src=src, ncols=ncols: e.copy(
                    stg[:, 0:ncols], src[:]), inc=("cS", 1),
                    waits=[("cV", w.mark[("v10", phid, tt)])])
                mk = w.n["cS"]
                w.do("sync", lambda e, nm=nm, ncols=ncols: e.dma_start(
                    out=t[nm + "_d"][:], in_=stg[:, 0:ncols]), inc=("dS", 16),
                    waits=[("cS", mk)])
                mk2 = w.n["dS"]
                w.do("scalar", lambda e: e.nop(), inc=("cS", 1),
                     waits=[("dS", mk2)])

    ntile = DEBUG_MAX_TILES if DEBUG_MAX_TILES is not None else TPS
    emit_ln(0)
    if ntile > 1:
        emit_ln(1)
    emit_qk_pe(0)
    emit_qk_v(0, 0)
    emit_qk_v(0, 1)
    emit_v(0)
    for tt in range(ntile):
        emit_scores(tt)
        if tt + 2 < ntile:
            emit_ln(tt + 2)
        if tt + 1 < ntile:
            emit_qk_pe(tt + 1)
        emit_tail(tt)
        if tt + 1 < ntile:
            emit_qk_v(tt + 1, 0)
        emit_tail_b(tt)
        if tt + 1 < ntile:
            emit_qk_v(tt + 1, 1)
            emit_v(tt + 1)
    if DEBUG_TAPB and phid == 1:
        stg = t["dbg_stg"]
        vm = w.mark[("v10", phid, ntile - 1)]
        w.do("sync", lambda e: e.dma_start(out=t["dbgB_gq_d"][:], in_=t["gq_rep"][:]),
             inc=("dS", 16), waits=[("cV", vm)])
        w.do("sync", lambda e: e.dma_start(out=t["dbgB_gk_d"][:], in_=t["gk_rep"][:]),
             inc=("dS", 16))
        items = [("dbgB_Gb_d", t["Gb"], 0)]
        for i, wn in enumerate(("wq", "wk", "wv", "wo")):
            for c in range(4):
                items.append((f"dbgB_{wn}_d", wT[i][:, c * 2048:(c + 1) * 2048], c))
        for nm, src, c in items:
            ncols = src.shape[1]
            w.do("scalar", lambda e, src=src, ncols=ncols: e.copy(
                stg[:, 0:ncols], src[:]), inc=("cS", 1), waits=[("cV", vm)])
            mk = w.n["cS"]
            w.do("sync", lambda e, nm=nm, c=c, ncols=ncols: e.dma_start(
                out=t[nm][:, c * 2048:c * 2048 + ncols], in_=stg[:, 0:ncols]),
                inc=("dS", 16), waits=[("cS", mk)])
            mk2 = w.n["dS"]
            w.do("scalar", lambda e: e.nop(), inc=("cS", 1), waits=[("dS", mk2)])



def build_nc():
    nc = bass.Bass("TRN2", target_bir_lowering=False)
    t = {}
    t["x_d"] = nc.dram_tensor("x", [2, SEQ, D], F32, kind="ExternalInput")
    t["out_d"] = nc.dram_tensor("out", [2, SEQ, D], F32, kind="ExternalOutput")
    for i in range(4):
        t[f"wt_dram{i}"] = nc.dram_tensor(f"wt_dram{i}", [128, 8 * D], BF16,
                                          kind="Internal")
    t["w1bf_dram"] = nc.dram_tensor("w1bf_dram", [128, 8 * HID], BF16, kind="Internal")
    t["w2bf_dram"] = nc.dram_tensor("w2bf_dram", [128, 32 * D], BF16, kind="Internal")
    for n, shp in (("wq", [D, D]), ("wk", [D, D]), ("wv", [D, D]), ("wo", [D, D]),
                   ("w1", [D, HID]), ("w2", [HID, D]),
                   ("b1r", [32, 128]), ("b2r", [1, D]),
                   ("gq", [1, D]), ("gk", [1, D]), ("G", [128, H * 144]),
                   ("eye", [128, 128])):
        t[n + "_d"] = nc.dram_tensor(n, shp, F32, kind="ExternalInput")
    if DEBUG_TAPB:
        t["dbgB_gq_d"] = nc.dram_tensor("dbgB_gq", [128, D], F32, kind="ExternalOutput")
        t["dbgB_gk_d"] = nc.dram_tensor("dbgB_gk", [128, D], F32, kind="ExternalOutput")
        t["dbgB_Gb_d"] = nc.dram_tensor("dbgB_Gb", [128, H * 144], F32, kind="ExternalOutput")
        for wn in ("wq", "wk", "wv", "wo"):
            t[f"dbgB_{wn}_d"] = nc.dram_tensor(f"dbgB_{wn}", [128, 8 * D], F32,
                                               kind="ExternalOutput")
    if DEBUG_TAP is not None:
        for n, shp in (("dbg_qn", [128, D]), ("dbg_kn", [128, D]),
                       ("dbg_vb", [128, D]), ("dbg_en", [128, 2048]),
                       ("dbg_rs", [128, 8]), ("dbg_afm", [128, D]),
                       ("dbg_xn", [128, D]), ("dbg_xnT", [128, D]),
                       ("dbg_eall", [128, H * 144])):
            t[n + "_d"] = nc.dram_tensor(n, shp, F32, kind="ExternalOutput")

    perm = ExitStack()

    def sb(name, shape, dtype, stk=perm):
        return stk.enter_context(nc.sbuf_tensor(name, shape, dtype))

    t["xres"] = sb("xres", [128, TPS * 1024], F32)
    t["b1f"] = sb("b1f", [128, 32], F32)
    t["b1tmp"] = sb("b1tmp", [32, 128], BF16)
    t["b1T"] = sb("b1T", [128, 32], BF16)
    t["b2row"] = sb("b2row", [1, D], BF16)
    t["onesb"] = sb("onesb", [1, 128], BF16)
    t["idb"] = sb("idb", [128, 128], BF16)
    for n in ("xsumA", "xsqA", "muA", "rstdA", "vtmpA", "vtmp2A", "xsqAe"):
        t[n] = sb(n, [128, TPS], F32)
    for n in ("xsumB", "xsqB", "muB", "ex2B", "mu2B", "varB", "lnB", "rstdB",
              "qssB", "kssB", "rqB", "rkB", "lnB2", "msB", "xsqBe"):
        t[n] = sb(n, [128, 2], F32)
    t["ssqe"] = sb("ssqe", [128, 4], F32)
    t["negm"] = sb("negm", [128, 16], F32)
    t["negam"] = sb("negam", [128, 16], F32)
    t["rs8"] = sb("rs8", [128, 8], F32)
    t["rr8"] = sb("rr8", [128, 8], F32)

    stkA = ExitStack()
    t["w1b"] = sb("w1b", [128, 8 * HID], BF16, stkA)
    t["w2b"] = sb("w2b", [128, 32 * D], BF16, stkA)
    t["xng"] = sb("xng", [128, 2048], BF16, stkA)
    t["xnTg"] = sb("xnTg", [128, 2048], BF16, stkA)
    t["hbuf"] = sb("hbuf", [128, 3 * 256], BF16, stkA)
    stkA.close()

    stkB = ExitStack()
    for n in ("wqT", "wkT", "wvT", "woT", "wstg"):
        t[n] = sb(n, [128, 8 * D], BF16, stkB)
    t["Gb"] = sb("Gb", [128, H * 144], BF16, stkB)
    t["gq_rep"] = sb("gq_rep", [128, D], F32, stkB)
    t["gk_rep"] = sb("gk_rep", [128, D], F32, stkB)
    for n in ("xnB", "qn", "qnT", "kn", "vb0", "vb1", "attn_fm", "junkB"):
        t[n] = sb(n, [128, D], BF16, stkB)
    t["xnTB"] = sb("xnTB", [128, 2 * D], BF16, stkB)
    t["knT"] = sb("knT", [128, 2 * D], BF16, stkB)
    t["e_all"] = sb("e_all", [128, H * 144], BF16, stkB)
    if DEBUG_TAP is not None or DEBUG_TAPB:
        t["dbg_stg"] = sb("dbg_stg", [128, 2048], F32, stkB)
    t["e_n"] = sb("e_n", [128, H * 256], BF16, stkB)
    t["eT"] = sb("eT", [128, H * 256], BF16, stkB)
    stkB.close()

    stkP0 = ExitStack()
    t["pst"] = [stkP0.enter_context(nc.psum_tensor(f"pst{i}", [128, 512], BF16))
                for i in range(2)]
    stkP0.close()
    stkPA = ExitStack()
    t["psh"] = [stkPA.enter_context(nc.psum_tensor(f"psh{i}", [128, 512], F32))
                for i in range(2)]
    t["pso"] = stkPA.enter_context(nc.psum_tensor("pso", [128, 2048], F32))
    stkPA.close()
    stkPB = ExitStack()
    for n in ("psq", "psk", "psv", "psx"):
        t[n] = stkPB.enter_context(nc.psum_tensor(n, [128, 1024], F32))
    stkPB.close()

    with ExitStack() as ctx:
        sems = {}
        for s in ("dW", "dX", "dT", "dS", "cV", "cS", "cP", "cB"):
            sems[s] = ctx.enter_context(nc.semaphore(s))
        block = ctx.enter_context(nc.Block())

        @block.sync
        def _(eng):
            build_walk(Walk("sync", eng, sems), t)

        @block.gpsimd
        def _(eng):
            build_walk(Walk("gpsimd", eng, sems), t)

        @block.tensor
        def _(eng):
            build_walk(Walk("tensor", eng, sems), t)

        @block.vector
        def _(eng):
            build_walk(Walk("vector", eng, sems), t)

        @block.scalar
        def _(eng):
            build_walk(Walk("scalar", eng, sems), t)

    perm.close()
    return nc


_NC_CACHE = None


def _get_nc():
    global _NC_CACHE
    if _NC_CACHE is None:
        _NC_CACHE = build_nc()
    return _NC_CACHE


def _build_G():
    slopes = np.exp2(-np.arange(1, H + 1, dtype=np.float64))
    G = np.zeros((128, H * 144), np.float32)
    i = np.arange(128)[:, None]
    j = np.arange(144)[None, :]
    band = (j >= i) & (j <= i + WIN)
    for h in range(H):
        G[:, h * 144:(h + 1) * 144] = np.where(
            band, np.exp(slopes[h] * (j - WIN - i)), 0.0).astype(np.float32)
    return G


_EYE = np.eye(128, dtype=np.float32)


def kernel(L0_attn_x, L0_attn_wq, L0_attn_wk, L0_attn_wv, L0_attn_wo,
           L0_attn_q_norm_w, L0_attn_k_norm_w,
           L0_ffn_w1, L0_ffn_b1, L0_ffn_w2, L0_ffn_b2):
    x = np.ascontiguousarray(np.asarray(L0_attn_x, np.float32))
    wq = np.ascontiguousarray(np.asarray(L0_attn_wq, np.float32))
    wk = np.ascontiguousarray(np.asarray(L0_attn_wk, np.float32))
    wv = np.ascontiguousarray(np.asarray(L0_attn_wv, np.float32))
    wo = np.ascontiguousarray(np.asarray(L0_attn_wo, np.float32))
    w1 = np.ascontiguousarray(np.asarray(L0_ffn_w1, np.float32))
    w2 = np.ascontiguousarray(np.asarray(L0_ffn_w2, np.float32))
    b1 = np.ascontiguousarray(np.asarray(L0_ffn_b1, np.float32).reshape(32, 128))
    b2 = np.ascontiguousarray(np.asarray(L0_ffn_b2, np.float32).reshape(1, D))
    gq = np.ascontiguousarray(np.asarray(L0_attn_q_norm_w, np.float32).reshape(1, D))
    gk = np.ascontiguousarray(np.asarray(L0_attn_k_norm_w, np.float32).reshape(1, D))
    G = _build_G()

    nc = _get_nc()
    in_maps = []
    for c in range(NCORES):
        in_maps.append(dict(
            x=np.ascontiguousarray(x[2 * c:2 * c + 2]),
            wq=wq, wk=wk, wv=wv, wo=wo, w1=w1, w2=w2,
            b1r=b1, b2r=b2, gq=gq, gk=gk, G=G, eye=_EYE))
    res = run_bass_kernel_spmd(nc, in_maps, core_ids=list(range(NCORES)))
    out = np.concatenate([np.asarray(r["out"]) for r in res.results], axis=0)
    return out.astype(np.float32)


# version 08
# speedup vs baseline: 1.0881x; 1.0095x over previous
import math
import numpy as np
from contextlib import ExitStack

import concourse.bass as bass
import concourse.mybir as mybir
from concourse.bass_utils import run_bass_kernel_spmd

F32 = mybir.dt.float32
BF16 = mybir.dt.bfloat16
AF = mybir.ActivationFunctionType
ALU = mybir.AluOpType
AX = mybir.AxisListType

NCORES = 8
SEQ = 2048
D = 1024
HID = 4096
H = 8
WIN = 16
TPS = 16
NG = 8
ASC = 1.0 / math.sqrt(128.0)
LNEPS = 1e-5
RMSEPS = 1e-6

ENGS = ("sync", "gpsimd", "tensor", "vector", "scalar")

PHASE_MASK = [True] * 6
DEBUG_TAP = None
DEBUG_MAX_TILES = None
DEBUG_TAPB = False


class Walk:

    def __init__(self, engname, eng, sems):
        self.engname = engname
        self.eng = eng
        self.sems = sems
        self.n = {k: 0 for k in sems}
        self.mark = {}
        self.barn = 0

    def do(self, who, fn, inc=None, waits=()):
        if self.engname == who:
            for sem, val in waits:
                if val:
                    self.eng.wait_ge(self.sems[sem], val)
            instr = fn(self.eng)
        else:
            instr = None
        if inc is not None:
            sem, amt = inc
            self.n[sem] += amt
            if instr is not None:
                instr.then_inc(self.sems[sem], amt)

    def setmark(self, key, sem):
        self.mark[key] = self.n[sem]
        return self.mark[key]

    def barrier(self):
        self.barn += 1
        for who in ("gpsimd", "tensor", "vector", "scalar"):
            self.do(who, lambda e: e.nop(), inc=("cB", 1))
        tgt = self.barn * 4
        if self.engname in ENGS:
            self.eng.wait_ge(self.sems["cB"], tgt)


def build_walk(w, t):
    v = dict(
        xng_tp=None,
        xnTg_mm=None,
        pso_ev=None,
        gelu_bank=[None, None],
        w1done=None, w2=[None] * 8,
        last_tp=None,
    )

    w.do("vector", lambda e: e.memset(t["onesb"][:], 1.0), inc=("cV", 1))
    w.setmark("ones", "cV")
    w.do("gpsimd", lambda e: e.dma_start(out=t["b1tmp"][:], in_=t["b1r_d"][:]),
         inc=("dW", 16))
    w.setmark("b1tmp", "dW")
    w.do("gpsimd", lambda e: e.dma_start(out=t["b2row"][:], in_=t["b2r_d"][:]),
         inc=("dW", 16))
    w.setmark("b2row", "dW")
    w.do("sync", lambda e: e.dma_start(out=t["b1T"][:], in_=t["b1tmp"][:],
                                       transpose=True),
         inc=("dT", 16), waits=[("dW", w.mark["b1tmp"])])
    w.setmark("b1T", "dT")
    w.do("scalar", lambda e: e.copy(t["b1f"][:], t["b1T"][:]),
         inc=("cS", 1), waits=[("dT", w.mark["b1T"])])
    xin = t["x_d"].rearrange("s (tt p) d -> s p tt d", p=128)
    xres3 = t["xres"].rearrange("p (tt d) -> p tt d", d=1024)
    for i in range(4):
        w.do("sync",
             lambda e, i=i: e.dma_start(out=xres3[:, i * 4:(i + 1) * 4, :],
                                        in_=xin[0, :, i * 4:(i + 1) * 4, :]),
             inc=("dX", 16))
        w.setmark(("xload", 0, i), "dX")
    w.do("gpsimd", lambda e: e.dma_start(out=t["idb"][:], in_=t["eye_d"][:]),
         inc=("dW", 16))
    w.setmark("idb", "dW")
    wnames = ("wq_d", "wk_d", "wv_d", "wo_d")
    pst = t["pst"]
    for i in range(4):
        wsrc = t[wnames[i]].rearrange("(oc p) d -> p oc d", p=128)
        wstg3 = t["wstg"].rearrange("p (oc d) -> p oc d", oc=8)
        guard = [("cP", w.mark.get(("wtp", i - 1, 15)))]
        w.do("gpsimd", lambda e, wsrc=wsrc, wstg3=wstg3: e.dma_start(
            out=wstg3[:], in_=wsrc[:]), inc=("dW", 16), waits=guard)
        w.setmark(("wstgs", i), "dW")
        for grp in range(16):
            dd, och = grp // 2, (grp % 2) * 4
            tpw = []
            if grp == 0:
                tpw = [("dW", w.mark[("wstgs", i)]), ("dW", w.mark["idb"])]
            cpm = w.mark.get(("wcp", i, grp - 2)) if grp >= 2 else \
                w.mark.get(("wcp", i - 1, 14 + grp))
            if cpm:
                tpw = tpw + [("cV", cpm)]
            for b in range(4):
                oc = och + b
                w.do("tensor", lambda e, dd=dd, oc=oc, b=b, grp=grp: e.transpose(
                    pst[grp % 2][:, b * 128:(b + 1) * 128],
                    t["wstg"][:, oc * 1024 + dd * 128:oc * 1024 + dd * 128 + 128],
                    t["idb"][:]),
                    inc=(("cP", 1) if b == 3 else None),
                    waits=tpw if b == 0 else [])
            w.setmark(("wtp", i, grp), "cP")
            cw = [("cP", w.mark[("wtp", i, grp)])]
            if grp == 0 and i > 0:
                cw.append(("dS", w.mark[("wTout", i - 1)]))
            w.do("vector", lambda e, dd=dd, och=och, grp=grp: e.tensor_copy(
                t["wqT"][:, dd * 1024 + (och // 4) * 512:
                         dd * 1024 + (och // 4) * 512 + 512],
                pst[grp % 2][:]),
                inc=("cV", 1), waits=cw)
            w.setmark(("wcp", i, grp), "cV")
        w.do("sync", lambda e, i=i: e.dma_start(
            out=t[f"wt_dram{i}"][:], in_=t["wqT"][:]),
            inc=("dS", 16), waits=[("cV", w.mark[("wcp", i, 15)])])
        w.setmark(("wTout", i), "dS")

    full = all(PHASE_MASK)
    xres3 = t["xres"].rearrange("p (tt d) -> p tt d", d=1024)
    outv = t["out_d"].rearrange("s (tt p) d -> s p tt d", p=128)
    for seq in range(2):
        if PHASE_MASK[seq * 3 + 0]:
            phase_ffn(w, t, v, seq, phid=seq * 3 + 0,
                      load_weights=(seq == 0) if full else True, store=False)
        if PHASE_MASK[seq * 3 + 1]:
            w.barrier()
            phase_attn(w, t, v, seq, phid=seq * 3 + 1)
            w.barrier()
        if PHASE_MASK[seq * 3 + 2]:
            phase_ffn(w, t, v, seq, phid=seq * 3 + 2,
                      load_weights=True, store=True)
        else:
            cv_now = w.n["cV"]
            w.do("sync", lambda e, seq=seq: e.dma_start(
                out=outv[seq], in_=xres3[:]),
                inc=("dS", 16), waits=[("cV", cv_now)])
            if seq == 0:
                w.do("sync", lambda e: e.dma_start(out=xres3[:], in_=xin[1]),
                     inc=("dX", 16))
                for g in range(NG):
                    w.setmark(("xload", 1, g), "dX")

    total_dS = w.n["dS"]
    if w.engname == "sync":
        w.eng.wait_ge(w.sems["dS"], total_dS)



def phase_ffn(w, t, v, seq, phid, load_weights, store):
    xres = t["xres"]
    xres3 = xres.rearrange("p (tt d) -> p tt d", d=1024)
    xng, xnTg, hbuf = t["xng"], t["xnTg"], t["hbuf"]
    w1b, w2b = t["w1b"], t["w2b"]
    psh, pso = t["psh"], t["pso"]
    xin = t["x_d"].rearrange("s (tt p) d -> s p tt d", p=128)
    outv = t["out_d"].rearrange("s (tt p) d -> s p tt d", p=128)

    if load_weights and phid == 0:
        guard = [("dS", w.mark[("wTout", 3)])]
        for k in range(8):
            w.do("gpsimd",
                 lambda e, k=k: e.dma_start(out=w1b[:, k * 4096:(k + 1) * 4096],
                                            in_=t["w1_d"][k * 128:(k + 1) * 128, :]),
                 inc=("dW", 16), waits=guard if k == 0 else [])
        v["w1done"] = w.setmark(("w1", phid), "dW")
        w2v = t["w2_d"].rearrange("(j c p) o -> j p c o", j=8, p=128)
        w2b4 = w2b.rearrange("p (j c o) -> p j c o", j=8, c=4)
        for j in range(8):
            w.do("gpsimd",
                 lambda e, j=j: e.dma_start(out=w2b4[:, j], in_=w2v[j]),
                 inc=("dW", 16))
            v["w2"][j] = w.setmark(("w2", phid, j), "dW")
    elif load_weights:
        guard = [("dT", v["last_tp"]), ("dS", w.mark["wbfout"])]
        w1v = w1b.rearrange("p (k m) -> p k m", k=8)
        w1dv = t["w1bf_dram"].rearrange("p (k m) -> p k m", k=8)
        for j in range(4):
            w.do("gpsimd", lambda e, j=j: e.dma_start(
                out=w1v[:, :, j * 1024:(j + 1) * 1024],
                in_=w1dv[:, :, j * 1024:(j + 1) * 1024]),
                inc=("dW", 16), waits=guard if j == 0 else [])
            w.setmark(("w1s", phid, j), "dW")
        v["w1done"] = w.mark[("w1s", phid, 0)]
        v["w1slice"] = [w.mark[("w1s", phid, j)] for j in range(4)]
        for j in range(4):
            w.do("gpsimd", lambda e, j=j: e.dma_start(
                out=w2b[:, j * 8192:(j + 1) * 8192],
                in_=t["w2bf_dram"][:, j * 8192:(j + 1) * 8192]),
                inc=("dW", 16))
            w.setmark(("w2s", phid, j), "dW")
            for jj in range(2):
                v["w2"][j * 2 + jj] = w.mark[("w2s", phid, j)]

    for tt in range(TPS):
        xt = xres3[:, tt, :]
        wv, ws = [], []
        if phid == 0:
            wv = [("dX", w.mark[("xload", 0, tt // 4)])]
            ws = list(wv)
        elif phid == 3:
            wv = [("dX", w.mark[("xload", 1, tt // 2)])]
            ws = list(wv)
        if tt == 0:
            ws = ws + [("dT", v["xng_tp"])]
        w.do("vector", lambda e, xt=xt, tt=tt: e.reduce_sum(
            t["xsumA"][:, tt:tt + 1], xt, axis=AX.X), inc=("cV", 1), waits=wv)
        w.do("scalar", lambda e, xt=xt, tt=tt: e.activation(
            out=xng[:, 0:1024], in_=xt, func=AF.Square,
            accum_out=t["xsqA"][:, tt:tt + 1]), inc=("cS", 1), waits=ws)
    w.do("scalar", lambda e: e.copy(t["xsqAe"][:], t["xsqA"][:]),
         inc=("cS", 1), waits=[("cS", w.n["cS"])])
    w.setmark(("a1s", phid), "cS")
    w.do("vector", lambda e: e.tensor_scalar_mul(t["muA"][:], t["xsumA"][:], 1.0 / D),
         inc=("cV", 1), waits=[("cS", w.mark[("a1s", phid)]), ("cV", w.n["cV"])])
    w.do("vector", lambda e: e.tensor_scalar(
        out=t["vtmpA"][:], in0=t["xsqAe"][:], scalar1=1.0 / D, scalar2=LNEPS,
        op0=ALU.mult, op1=ALU.add), inc=("cV", 1))
    w.do("vector", lambda e: e.tensor_mul(t["vtmp2A"][:], t["muA"][:], t["muA"][:]),
         inc=("cV", 1), waits=[("cV", w.n["cV"])])
    w.do("vector", lambda e: e.tensor_sub(t["vtmpA"][:], t["vtmpA"][:], t["vtmp2A"][:]),
         inc=("cV", 1), waits=[("cV", w.n["cV"])])
    w.setmark(("varA", phid), "cV")
    w.do("scalar", lambda e: e.activation(out=t["vtmp2A"][:], in_=t["vtmpA"][:],
                                          func=AF.Ln),
         inc=("cS", 1), waits=[("cV", w.mark[("varA", phid)])])
    w.do("scalar", lambda e: e.activation(out=t["rstdA"][:], in_=t["vtmp2A"][:],
                                          func=AF.Exp, scale=-0.5), inc=("cS", 1),
         waits=[("cS", w.n["cS"])])
    w.setmark(("rstdA", phid), "cS")

    xn_w = xng.rearrange("p (k two f) -> p k two f", k=8, two=2, f=128)
    for g in range(NG):
        for sub in range(2):
            tt = 2 * g + sub
            waits = []
            if sub == 0:
                waits.append(("cS", w.mark[("rstdA", phid)]))
                waits.append(("dT", v["xng_tp"]))
                waits.append(("cV", w.n["cV"]))
            w.do("vector", lambda e, tt=tt, sub=sub: e.tensor_scalar(
                out=xn_w[:, :, sub, :], in0=xres3[:, tt, :],
                scalar1=t["muA"][:, tt:tt + 1], scalar2=t["rstdA"][:, tt:tt + 1],
                op0=ALU.subtract, op1=ALU.mult), inc=("cV", 1), waits=waits)
        w.setmark(("Axn", phid, g), "cV")
        tw = [("cV", w.mark[("Axn", phid, g)]), ("cP", v["xnTg_mm"])]
        w.do("sync", lambda e: e.dma_start(
            out=xnTg.rearrange("p (b q) -> p b q", b=16),
            in_=xng[:], transpose=True), inc=("dT", 16), waits=tw)
        v["xng_tp"] = w.setmark(("AxnT", phid, g), "dT")
        v["last_tp"] = v["xng_tp"]
        if g > 0:
            emit_ffn_evac(w, t, v, seq, phid, g - 1, store, outv, xres3, xin)

        for m in range(32):
            pw = [("cS", v["gelu_bank"][m % 2])]
            if m == 0:
                pw.append(("dT", w.mark[("AxnT", phid, g)]))
            if g == 0:
                sl = v.get("w1slice")
                if sl and m % 8 == 0:
                    pw.append(("dW", sl[m // 8]))
                elif m == 0:
                    pw.append(("dW", v["w1done"]))
            bank = psh[m % 2]
            for k in range(8):
                w.do("tensor", lambda e, k=k, m=m, bank=bank: e.matmul(
                    bank[:, 0:256],
                    w1b[:, k * 4096 + m * 128:k * 4096 + (m + 1) * 128],
                    xnTg[:, k * 256:(k + 1) * 256],
                    start=(k == 0), stop=(k == 7)),
                    inc=(("cP", 1) if k == 7 else None),
                    waits=pw if k == 0 else [])
            w.setmark(("Ah", phid, g, m), "cP")
            w.do("scalar", lambda e, m=m, bank=bank: e.activation(
                out=hbuf[:, (m % 3) * 256:(m % 3 + 1) * 256], in_=bank[:, 0:256],
                func=AF.Gelu, bias=t["b1f"][:, m:m + 1]),
                inc=("cS", 1), waits=[("cP", w.mark[("Ah", phid, g, m)])])
            v["gelu_bank"][m % 2] = w.setmark(("Agelu", phid, g, m), "cS")
            if m >= 1:
                emit_mm2(w, t, v, phid, g, m - 1)
        emit_mm2(w, t, v, phid, g, 31)
        v["xnTg_mm"] = w.mark[("Ah", phid, g, 31)]
        for sub in range(2):
            for nh in range(2):
                w.do("tensor", lambda e, sub=sub, nh=nh: e.matmul(
                    pso[:, sub * 1024 + nh * 512:sub * 1024 + nh * 512 + 512],
                    t["onesb"][:], t["b2row"][:, nh * 512:nh * 512 + 512],
                    start=False, stop=True),
                    inc=(("cP", 1) if (sub, nh) == (1, 1) else None),
                    waits=[("dW", w.mark["b2row"]), ("cV", w.mark["ones"])]
                    if (sub, nh) == (0, 0) else [])
        w.setmark(("Aout", phid, g), "cP")
    emit_ffn_evac(w, t, v, seq, phid, NG - 1, store, outv, xres3, xin)
    if load_weights and phid == 0:
        w.do("sync", lambda e: e.dma_start(out=t["w1bf_dram"][:], in_=w1b[:]),
             inc=("dS", 16), waits=[("dW", v["w1done"])])
        w.do("sync", lambda e: e.dma_start(out=t["w2bf_dram"][:], in_=w2b[:]),
             inc=("dS", 16), waits=[("dW", v["w2"][7])])
        w.setmark("wbfout", "dS")


def emit_mm2(w, t, v, phid, g, m):
    pso, hbuf, w2b = t["pso"], t["hbuf"], t["w2b"]
    pw = [("cS", w.mark[("Agelu", phid, g, m)])]
    if m == 0:
        pw.append(("cV", v["pso_ev"]))
    if g == 0 and v["w2"][m // 4]:
        pw.append(("dW", v["w2"][m // 4]))
    for sub in range(2):
        for nh in range(2):
            w.do("tensor", lambda e, sub=sub, nh=nh, m=m: e.matmul(
                pso[:, sub * 1024 + nh * 512:sub * 1024 + nh * 512 + 512],
                hbuf[:, (m % 3) * 256 + sub * 128:(m % 3) * 256 + sub * 128 + 128],
                w2b[:, m * 1024 + nh * 512:m * 1024 + nh * 512 + 512],
                start=(m == 0), stop=False),
                waits=pw if (sub, nh) == (0, 0) else [])


def emit_ffn_evac(w, t, v, seq, phid, g, store, outv, xres3, xin):
    pso = t["pso"]
    for sub in range(2):
        tt = 2 * g + sub
        w.do("vector", lambda e, tt=tt, sub=sub: e.scalar_tensor_tensor(
            out=xres3[:, tt, :], in0=pso[:, sub * 1024:(sub + 1) * 1024],
            scalar=0.5, in1=xres3[:, tt, :],
            op0=ALU.mult, op1=ALU.add),
            inc=("cV", 1),
            waits=[("cP", w.mark[("Aout", phid, g)])] if sub == 0 else [])
    v["pso_ev"] = w.setmark(("Aevac", phid, g), "cV")
    if store:
        w.do("sync", lambda e, g=g: e.dma_start(
            out=outv[seq, :, 2 * g:2 * g + 2, :],
            in_=xres3[:, 2 * g:2 * g + 2, :]),
            inc=("dS", 16), waits=[("cV", w.mark[("Aevac", phid, g)])])
        if seq == 0:
            w.do("sync", lambda e, g=g: e.dma_start(
                out=xres3[:, 2 * g:2 * g + 2, :],
                in_=xin[1, :, 2 * g:2 * g + 2, :]), inc=("dX", 16))
            w.setmark(("xload", 1, g), "dX")



def phase_attn(w, t, v, seq, phid):
    xres3 = t["xres"].rearrange("p (tt d) -> p tt d", d=1024)
    psq, psk, psv, psx = t["psq"], t["psk"], t["psv"], t["psx"]
    xnB = t["xnB"]
    qn, qnT, kn, knT = t["qn"], t["qnT"], t["kn"], t["knT"]
    vb = (t["vb0"], t["vb1"])
    e_all, e_n, eT, afm = t["e_all"], t["e_n"], t["eT"], t["attn_fm"]
    wT = (t["wqT"], t["wkT"], t["wvT"], t["woT"])
    junk = t["junkB"]
    region_guard = v["last_tp"]
    knT_w = knT.rearrange("p (h s q) -> p h s q", h=8, s=2, q=128)
    xnT_w = t["xnTB"].rearrange("p (s b q) -> p s b q", s=2, b=8)

    for i in range(4):
        guard = [("dS", w.mark[("wTout", 3)])]
        if i == 0:
            guard.append(("dT", region_guard))
        w.do("gpsimd", lambda e, i=i: e.dma_start(
            out=wT[i][:], in_=t[f"wt_dram{i}"][:]), inc=("dW", 16),
            waits=guard if i == 0 or i == 1 else [])
        w.setmark(("wT", phid, i), "dW")
    w.do("gpsimd", lambda e: e.dma_start(out=t["Gb"][:], in_=t["G_d"][:]),
         inc=("dW", 16))
    w.setmark(("gb", phid), "dW")
    for gi, gd, rep in ((0, t["gq_d"], t["gq_rep"]), (1, t["gk_d"], t["gk_rep"])):
        w.do("gpsimd", lambda e, gd=gd, rep=rep: e.dma_start(
            out=rep[:], in_=bass.AP(gd.tensor if hasattr(gd, "tensor") else gd,
                                    0, [[0, 128], [1, D]])),
            inc=("dW", 16))
        w.setmark(("repv", phid, gi), "dW")
    w.do("vector", lambda e: e.memset(knT[:], 0.0), inc=("cV", 1),
         waits=[("dT", region_guard)])
    w.do("vector", lambda e: e.memset(vb[0][:], 0.0), inc=("cV", 1))
    w.do("vector", lambda e: e.memset(vb[1][:], 0.0), inc=("cV", 1))
    w.do("vector", lambda e: e.memset(e_n[:], 0.0), inc=("cV", 1))
    w.setmark(("bmemset", phid), "cV")
    if w.engname in ("tensor", "vector", "scalar"):
        w.eng.wait_ge(w.sems["dW"], w.mark[("repv", phid, 1)])
    w.barrier()

    def emit_ln(tt):
        par = tt % 2
        xt = xres3[:, tt, :]
        ldw = []
        if tt <= 1:
            ldw = [("dX", w.mark.get(("xload", seq, 3 if seq == 0 else NG - 1)))]
        w.do("vector", lambda e: e.reduce_sum(
            t["xsumB"][:, par:par + 1], xt, axis=AX.X), inc=("cV", 1), waits=ldw)
        w.setmark(("v1", phid, tt), "cV")
        sq_w = ([("dT", region_guard)] if tt <= 1 else []) + ldw
        w.do("scalar", lambda e: e.activation(
            out=junk[:], in_=xt, func=AF.Square,
            accum_out=t["xsqB"][:, par:par + 1]), inc=("cS", 1), waits=sq_w)
        w.do("scalar", lambda e: e.copy(
            t["xsqBe"][:, par:par + 1], t["xsqB"][:, par:par + 1]),
            inc=("cS", 1), waits=[("cS", w.n["cS"])])
        w.setmark(("s1", phid, tt), "cS")
        w.do("vector", lambda e: e.tensor_scalar_mul(
            t["muB"][:, par:par + 1], t["xsumB"][:, par:par + 1], 1.0 / D),
            inc=("cV", 1), waits=[("cS", w.mark[("s1", phid, tt)]),
                                  ("cV", w.mark[("v1", phid, tt)])])
        w.do("vector", lambda e: e.tensor_scalar(
            out=t["ex2B"][:, par:par + 1], in0=t["xsqBe"][:, par:par + 1],
            scalar1=1.0 / D, scalar2=LNEPS, op0=ALU.mult, op1=ALU.add),
            inc=("cV", 1))
        w.do("vector", lambda e: e.tensor_mul(
            t["mu2B"][:, par:par + 1], t["muB"][:, par:par + 1],
            t["muB"][:, par:par + 1]), inc=("cV", 1), waits=[("cV", w.n["cV"])])
        w.do("vector", lambda e: e.tensor_sub(
            t["varB"][:, par:par + 1], t["ex2B"][:, par:par + 1],
            t["mu2B"][:, par:par + 1]), inc=("cV", 1), waits=[("cV", w.n["cV"])])
        w.setmark(("v1b", phid, tt), "cV")
        w.do("scalar", lambda e: e.activation(
            out=t["lnB"][:, par:par + 1], in_=t["varB"][:, par:par + 1],
            func=AF.Ln), inc=("cS", 1), waits=[("cV", w.mark[("v1b", phid, tt)])])
        w.do("scalar", lambda e: e.activation(
            out=t["rstdB"][:, par:par + 1], in_=t["lnB"][:, par:par + 1],
            func=AF.Exp, scale=-0.5), inc=("cS", 1), waits=[("cS", w.n["cS"])])
        w.setmark(("s2", phid, tt), "cS")
        xw = [("cS", w.mark[("s2", phid, tt)]),
              ("dT", w.mark.get(("xnT", phid, tt - 1)))]
        w.do("vector", lambda e: e.tensor_scalar(
            out=xnB[:], in0=xt, scalar1=t["muB"][:, par:par + 1],
            scalar2=t["rstdB"][:, par:par + 1],
            op0=ALU.subtract, op1=ALU.mult), inc=("cV", 1), waits=xw)
        w.setmark(("v2", phid, tt), "cV")
        tw = [("cV", w.mark[("v2", phid, tt)]),
              ("cP", w.mark.get(("p3", phid, tt - 2)))]
        w.do("sync", lambda e: e.dma_start(
            out=xnT_w[:, par], in_=xnB[:], transpose=True),
            inc=("dT", 16), waits=tw)
        v["last_tp"] = w.setmark(("xnT", phid, tt), "dT")

    def emit_qk_pe(tt):
        par = tt % 2
        for nm, ps, wTi, ssq, srs in (
                ("q", psq, t["wqT"], t["qssB"], t["rqB"]),
                ("k", psk, t["wkT"], t["kssB"], t["rkB"])):
            isq = nm == "q"
            pw = [("dT", w.mark[("xnT", phid, tt)])]
            if tt == 0:
                pw.append(("dW", w.mark[("wT", phid, 0 if isq else 1)]))
                pw.append(("dW", w.mark[("repv", phid, 1)]))
            else:
                if isq:
                    pw.append(("cV", w.mark[("v6", phid, tt - 1, 0)]))
                    pw.append(("cS", w.mark[("s7", phid, tt - 1)]))
                else:
                    pw.append(("cV", w.mark[("v6", phid, tt - 1, 1)]))
                    pw.append(("cS", w.mark[("s8", phid, tt - 1)]))
            for nh in range(2):
                for k in range(8):
                    w.do("tensor", lambda e, ps=ps, wTi=wTi, nh=nh, k=k: e.matmul(
                        ps[:, nh * 512:(nh + 1) * 512],
                        xnT_w[:, par, k, :],
                        wTi[:, k * 1024 + nh * 512:k * 1024 + nh * 512 + 512],
                        start=(k == 0), stop=(k == 7)),
                        inc=(("cP", 1) if (nh, k) == (1, 7) else None),
                        waits=pw if (nh, k) == (0, 0) else [])
            w.setmark(("p1" if isq else "p2", phid, tt), "cP")
            w.do("scalar", lambda e, ps=ps, ssq=ssq: e.activation(
                out=junk[:], in_=ps[:], func=AF.Square,
                accum_out=ssq[:, par:par + 1]), inc=("cS", 1),
                waits=[("cP", w.mark[("p1" if isq else "p2", phid, tt)])])
            ec = par * 2 + (0 if isq else 1)
            w.do("scalar", lambda e, ssq=ssq, ec=ec: e.copy(
                t["ssqe"][:, ec:ec + 1], ssq[:, par:par + 1]),
                inc=("cS", 1), waits=[("cS", w.n["cS"])])
            w.setmark(("s3" if isq else "s5", phid, tt), "cS")

    def emit_qk_v(tt, part):
        par = tt % 2
        for nm, ps, srs, grep, dst in (
                ("q", psq, t["rqB"], t["gq_rep"], qn),
                ("k", psk, t["rkB"], t["gk_rep"], kn)):
            isq = nm == "q"
            if part == 1:
                emit_qk_tp(tt, isq)
                continue
            ec = par * 2 + (0 if isq else 1)
            w.do("vector", lambda e, ec=ec: e.tensor_scalar(
                out=t["msB"][:, par:par + 1], in0=t["ssqe"][:, ec:ec + 1],
                scalar1=1.0 / D, scalar2=RMSEPS, op0=ALU.mult, op1=ALU.add),
                inc=("cV", 1), waits=[("cS", w.mark[("s3" if isq else "s5",
                                                     phid, tt)])])
            w.setmark(("vms", phid, tt, nm), "cV")
            w.do("scalar", lambda e: e.activation(
                out=t["lnB2"][:, par:par + 1], in_=t["msB"][:, par:par + 1],
                func=AF.Ln), inc=("cS", 1),
                waits=[("cV", w.mark[("vms", phid, tt, nm)])])
            w.do("scalar", lambda e, srs=srs: e.activation(
                out=srs[:, par:par + 1], in_=t["lnB2"][:, par:par + 1],
                func=AF.Exp, scale=-0.5), inc=("cS", 1),
                waits=[("cS", w.n["cS"])])
            w.setmark(("s4" if isq else "s6", phid, tt), "cS")
            vw = [("cS", w.mark[("s4" if isq else "s6", phid, tt)]),
                  ("dT", w.mark.get(("qnT" if isq else "knT", phid, tt - 1)))]
            w.do("vector", lambda e, ps=ps, dst=dst, srs=srs, grep=grep:
                 e.scalar_tensor_tensor(
                     out=dst[:], in0=ps[:], scalar=srs[:, par:par + 1],
                     in1=grep[:], op0=ALU.mult, op1=ALU.mult),
                 inc=("cV", 1), waits=vw)
            w.setmark(("v3" if isq else "v4", phid, tt), "cV")

    def emit_qk_tp(tt, isq):
        par = tt % 2
        tw = [("cV", w.mark[("v3" if isq else "v4", phid, tt)]),
              ("cP", w.mark.get(("p5", phid, tt - 1)))]
        if isq:
            w.do("sync", lambda e: e.dma_start(
                out=qnT.rearrange("p (b q) -> p b q", b=8), in_=qn[:],
                transpose=True), inc=("dT", 16), waits=tw)
            v["last_tp"] = w.setmark(("qnT", phid, tt), "dT")
        else:
            w.do("sync", lambda e: e.dma_start(
                out=knT_w[:, :, par, :], in_=kn[:],
                transpose=True), inc=("dT", 16), waits=tw)
            v["last_tp"] = w.setmark(("knT", phid, tt), "dT")

    def emit_v(tt):
        par = tt % 2
        pw = [("dT", w.mark[("xnT", phid, tt)])]
        if tt == 0:
            pw.append(("dW", w.mark[("wT", phid, 2)]))
            pw.append(("cV", w.mark[("bmemset", phid)]))
        else:
            pw.append(("cV", w.mark[("v9", phid, tt - 1)]))
        for nh in range(2):
            for k in range(8):
                w.do("tensor", lambda e, nh=nh, k=k: e.matmul(
                    psv[:, nh * 512:(nh + 1) * 512],
                    xnT_w[:, par, k, :],
                    t["wvT"][:, k * 1024 + nh * 512:k * 1024 + nh * 512 + 512],
                    start=(k == 0), stop=(k == 7)),
                    inc=(("cP", 1) if (nh, k) == (1, 7) else None),
                    waits=pw if (nh, k) == (0, 0) else [])
        w.setmark(("p3", phid, tt), "cP")
        w.do("vector", lambda e, par=par: e.tensor_copy(vb[par][:], psv[:]),
             inc=("cV", 1),
             waits=[("cP", w.mark[("p3", phid, tt)]),
                    ("cP", w.mark.get(("p6", phid, tt - 1)))])
        w.setmark(("v5", phid, tt), "cV")

    def emit_scores(tt):
        par = tt % 2
        cur, prev = tt % 2, (tt + 1) % 2
        for rnd, psS in ((0, psq), (1, psk)):
            pw = [("dT", w.mark[("knT", phid, tt)]),
                  ("dT", w.mark[("qnT", phid, tt)])]
            if rnd == 0:
                pw.append(("cV", w.mark[("v3", phid, tt)]))
                pw.append(("cS", w.mark[("s3", phid, tt)]))
            else:
                pw.append(("cV", w.mark[("v4", phid, tt)]))
                pw.append(("cS", w.mark[("s5", phid, tt)]))
            for hh in range(4):
                h = rnd * 4 + hh
                w.do("tensor", lambda e, psS=psS, hh=hh, h=h: e.matmul(
                    psS[:, hh * 256 + 112:hh * 256 + 128],
                    qnT[:, h * 128:(h + 1) * 128],
                    knT_w[:, h, prev, 112:128], start=True, stop=True),
                    waits=pw if hh == 0 else [])
                w.do("tensor", lambda e, psS=psS, hh=hh, h=h: e.matmul(
                    psS[:, hh * 256 + 128:hh * 256 + 256],
                    qnT[:, h * 128:(h + 1) * 128],
                    knT_w[:, h, cur, :], start=True, stop=True),
                    inc=(("cP", 1) if hh == 3 else None))
            w.setmark((("p4", "p5")[rnd], phid, tt), "cP")
            w.do("vector", lambda e, psS=psS, rnd=rnd: e.reduce_max(
                t["negm"][:, par * 8 + rnd * 4:par * 8 + rnd * 4 + 4],
                psS.rearrange("p (hh c) -> p hh c", hh=4)[:, :, 112:256],
                axis=AX.X, negate=True),
                inc=("cV", 1),
                waits=[("cP", w.mark[(("p4", "p5")[rnd], phid, tt)])])
            w.do("vector", lambda e, rnd=rnd: e.tensor_scalar_mul(
                t["negam"][:, par * 8 + rnd * 4:par * 8 + rnd * 4 + 4],
                t["negm"][:, par * 8 + rnd * 4:par * 8 + rnd * 4 + 4], ASC),
                inc=("cV", 1), waits=[("cV", w.n["cV"])])
            w.setmark(("v6", phid, tt, rnd), "cV")
            sw = [("cV", w.mark[("v6", phid, tt, rnd)])]
            if rnd == 0:
                sw.append(("cV", w.mark.get(("v7", phid, tt - 1))))
            for hh in range(4):
                h = rnd * 4 + hh
                w.do("scalar", lambda e, psS=psS, hh=hh, h=h: e.activation(
                    out=e_all[:, h * 144:(h + 1) * 144],
                    in_=psS[:, hh * 256 + 112:hh * 256 + 256],
                    func=AF.Exp, scale=ASC,
                    bias=t["negam"][:, par * 8 + h:par * 8 + h + 1]),
                    inc=(("cS", 1) if hh == 3 else None),
                    waits=sw if hh == 0 else [])
            w.setmark((("s7", "s8")[rnd], phid, tt), "cS")

    def emit_tail(tt):
        par = tt % 2
        cur, prev = tt % 2, (tt + 1) % 2
        xt = xres3[:, tt, :]
        vw = [("cS", w.mark[("s8", phid, tt)]),
              ("dT", w.mark.get(("eT", phid, tt - 1)))]
        if tt == 0:
            vw.append(("dW", w.mark[("gb", phid)]))

        def v7_ops(hs):
            for h in hs:
                w.do("vector", lambda e, h=h: e.scalar_tensor_tensor(
                    out=e_n[:, h * 256 + 112:h * 256 + 256],
                    in0=e_all[:, h * 144:(h + 1) * 144], scalar=1.0,
                    in1=t["Gb"][:, h * 144:(h + 1) * 144],
                    op0=ALU.mult, op1=ALU.mult,
                    accum_out=t["rs8"][:, h:h + 1]),
                    inc=(("cV", 1) if h == hs[-1] else None),
                    waits=vw if h == 0 else [])

        def v8_ops(hs, half):
            w.do("vector", lambda e, hs=hs: e.reciprocal(
                t["rr8"][:, hs[0]:hs[-1] + 1], t["rs8"][:, hs[0]:hs[-1] + 1]),
                inc=("cV", 1), waits=[("cV", w.n["cV"])])
            for h in hs:
                w.do("vector", lambda e, h=h: e.tensor_scalar_mul(
                    e_n[:, h * 256 + 112:h * 256 + 256],
                    e_n[:, h * 256 + 112:h * 256 + 256],
                    t["rr8"][:, h:h + 1]), inc=(("cV", 1) if h == hs[-1] else None),
                    waits=[("cV", w.n["cV"])] if h == hs[0] else [])
            w.setmark(("v8a" if half == 0 else "v8", phid, tt), "cV")
            tw = [("cV", w.mark[("v8a" if half == 0 else "v8", phid, tt)])]
            if half == 0:
                tw.append(("cP", w.mark.get(("p6", phid, tt - 1))))
            eT3 = eT.rearrange("p (b q) -> p b q", b=16)
            w.do("sync", lambda e, half=half, eT3=eT3: e.dma_start(
                out=eT3[:, half * 8:half * 8 + 8, :],
                in_=e_n[:, half * 1024:half * 1024 + 1024],
                transpose=True), inc=("dT", 16), waits=tw)
            v["last_tp"] = w.setmark(("eT0" if half == 0 else "eT", phid, tt),
                                     "dT")

        if tt == 0:
            v7_ops(list(range(8)))
            w.setmark(("v7", phid, tt), "cV")
            for h in range(8):
                w.do("vector", lambda e, h=h: e.memset(
                    e_n[:, h * 256 + 112:h * 256 + 128], 0.0), inc=("cV", 1))
            w.do("vector", lambda e: e.reduce_sum(
                t["rs8"][:], e_n.rearrange("p (h c) -> p h c", h=8), axis=AX.X),
                inc=("cV", 1), waits=[("cV", w.n["cV"])])
            v8_ops([0, 1, 2, 3], 0)
            v8_ops([4, 5, 6, 7], 1)
        else:
            v7_ops([0, 1, 2, 3])
            v8_ops([0, 1, 2, 3], 0)
            v7_ops([4, 5, 6, 7])
            w.setmark(("v7", phid, tt), "cV")
            v8_ops([4, 5, 6, 7], 1)
        pw = [("dT", w.mark[("eT0", phid, tt)]),
              ("cV", w.mark[("v5", phid, tt)])]
        pw2 = [("dT", w.mark[("eT", phid, tt)])]
        for h in range(8):
            w.do("tensor", lambda e, h=h: e.matmul(
                psv[:, h * 128:(h + 1) * 128], vb[prev][:, h * 128:(h + 1) * 128],
                eT[:, (2 * h) * 128:(2 * h) * 128 + 128],
                start=True, stop=False),
                waits=pw if h == 0 else (pw2 if h == 4 else []))
            w.do("tensor", lambda e, h=h: e.matmul(
                psv[:, h * 128:(h + 1) * 128], vb[cur][:, h * 128:(h + 1) * 128],
                eT[:, (2 * h + 1) * 128:(2 * h + 1) * 128 + 128],
                start=False, stop=True),
                inc=(("cP", 1) if h in (3, 7) else None))
            if h == 3:
                w.setmark(("p6a", phid, tt), "cP")
        w.setmark(("p6", phid, tt), "cP")

    def emit_tail_b(tt):
        par = tt % 2
        cur, prev = tt % 2, (tt + 1) % 2
        xt = xres3[:, tt, :]
        w.do("vector", lambda e: e.tensor_copy(afm[:, 0:512], psv[:, 0:512]),
             inc=("cV", 1),
             waits=[("cP", w.mark[("p6a", phid, tt)]),
                    ("cP", w.mark.get(("p7", phid, tt - 1)))])
        w.setmark(("v9a", phid, tt), "cV")
        w.do("vector", lambda e: e.tensor_copy(afm[:, 512:1024], psv[:, 512:1024]),
             inc=("cV", 1),
             waits=[("cP", w.mark[("p6", phid, tt)])])
        w.setmark(("v9", phid, tt), "cV")
        pw = [("cV", w.mark[("v9a", phid, tt)]),
              ("cV", w.mark.get(("v10", phid, tt - 1)))]
        if tt == 0:
            pw.append(("dW", w.mark[("wT", phid, 3)]))
        pw2 = [("cV", w.mark[("v9", phid, tt)])]
        for nh in range(2):
            for h in range(8):
                w.do("tensor", lambda e, nh=nh, h=h: e.matmul(
                    psx[:, nh * 512:(nh + 1) * 512], afm[:, h * 128:(h + 1) * 128],
                    t["woT"][:, h * 1024 + nh * 512:h * 1024 + nh * 512 + 512],
                    start=(h == 0), stop=(h == 7)),
                    inc=(("cP", 1) if (nh, h) == (1, 7) else None),
                    waits=pw if (nh, h) == (0, 0) else
                          (pw2 if (nh, h) == (0, 4) else []))
        w.setmark(("p7", phid, tt), "cP")
        w.do("vector", lambda e: e.scalar_tensor_tensor(
            out=xt, in0=psx[:], scalar=1.0, in1=xt,
            op0=ALU.mult, op1=ALU.add), inc=("cV", 1),
            waits=[("cP", w.mark[("p7", phid, tt)])])
        w.setmark(("v10", phid, tt), "cV")
        if DEBUG_TAP == (phid, tt):
            stg = t["dbg_stg"]
            for nm, src in (("dbg_qn", qn), ("dbg_kn", kn), ("dbg_vb", vb[cur]),
                            ("dbg_en", e_n), ("dbg_rs", t["rs8"]),
                            ("dbg_afm", afm), ("dbg_xn", xnB),
                            ("dbg_xnT", t["xnTB"][:, 0:1024]),
                            ("dbg_eall", e_all)):
                ncols = src.shape[1]
                w.do("scalar", lambda e, src=src, ncols=ncols: e.copy(
                    stg[:, 0:ncols], src[:]), inc=("cS", 1),
                    waits=[("cV", w.mark[("v10", phid, tt)])])
                mk = w.n["cS"]
                w.do("sync", lambda e, nm=nm, ncols=ncols: e.dma_start(
                    out=t[nm + "_d"][:], in_=stg[:, 0:ncols]), inc=("dS", 16),
                    waits=[("cS", mk)])
                mk2 = w.n["dS"]
                w.do("scalar", lambda e: e.nop(), inc=("cS", 1),
                     waits=[("dS", mk2)])

    ntile = DEBUG_MAX_TILES if DEBUG_MAX_TILES is not None else TPS
    emit_ln(0)
    if ntile > 1:
        emit_ln(1)
    emit_qk_pe(0)
    emit_qk_v(0, 0)
    emit_qk_v(0, 1)
    emit_v(0)
    for tt in range(ntile):
        emit_scores(tt)
        if tt + 2 < ntile:
            emit_ln(tt + 2)
        if tt + 1 < ntile:
            emit_qk_pe(tt + 1)
        emit_tail(tt)
        if tt + 1 < ntile:
            emit_qk_v(tt + 1, 0)
        emit_tail_b(tt)
        if tt + 1 < ntile:
            emit_qk_v(tt + 1, 1)
            emit_v(tt + 1)
    if DEBUG_TAPB and phid == 1:
        stg = t["dbg_stg"]
        vm = w.mark[("v10", phid, ntile - 1)]
        w.do("sync", lambda e: e.dma_start(out=t["dbgB_gq_d"][:], in_=t["gq_rep"][:]),
             inc=("dS", 16), waits=[("cV", vm)])
        w.do("sync", lambda e: e.dma_start(out=t["dbgB_gk_d"][:], in_=t["gk_rep"][:]),
             inc=("dS", 16))
        items = [("dbgB_Gb_d", t["Gb"], 0)]
        for i, wn in enumerate(("wq", "wk", "wv", "wo")):
            for c in range(4):
                items.append((f"dbgB_{wn}_d", wT[i][:, c * 2048:(c + 1) * 2048], c))
        for nm, src, c in items:
            ncols = src.shape[1]
            w.do("scalar", lambda e, src=src, ncols=ncols: e.copy(
                stg[:, 0:ncols], src[:]), inc=("cS", 1), waits=[("cV", vm)])
            mk = w.n["cS"]
            w.do("sync", lambda e, nm=nm, c=c, ncols=ncols: e.dma_start(
                out=t[nm][:, c * 2048:c * 2048 + ncols], in_=stg[:, 0:ncols]),
                inc=("dS", 16), waits=[("cS", mk)])
            mk2 = w.n["dS"]
            w.do("scalar", lambda e: e.nop(), inc=("cS", 1), waits=[("dS", mk2)])



def build_nc():
    nc = bass.Bass("TRN2", target_bir_lowering=False)
    t = {}
    t["x_d"] = nc.dram_tensor("x", [2, SEQ, D], F32, kind="ExternalInput")
    t["out_d"] = nc.dram_tensor("out", [2, SEQ, D], F32, kind="ExternalOutput")
    for i in range(4):
        t[f"wt_dram{i}"] = nc.dram_tensor(f"wt_dram{i}", [128, 8 * D], BF16,
                                          kind="Internal")
    t["w1bf_dram"] = nc.dram_tensor("w1bf_dram", [128, 8 * HID], BF16, kind="Internal")
    t["w2bf_dram"] = nc.dram_tensor("w2bf_dram", [128, 32 * D], BF16, kind="Internal")
    for n, shp in (("wq", [D, D]), ("wk", [D, D]), ("wv", [D, D]), ("wo", [D, D]),
                   ("w1", [D, HID]), ("w2", [HID, D]),
                   ("b1r", [32, 128]), ("b2r", [1, D]),
                   ("gq", [1, D]), ("gk", [1, D]), ("G", [128, H * 144]),
                   ("eye", [128, 128])):
        t[n + "_d"] = nc.dram_tensor(n, shp, F32, kind="ExternalInput")
    if DEBUG_TAPB:
        t["dbgB_gq_d"] = nc.dram_tensor("dbgB_gq", [128, D], F32, kind="ExternalOutput")
        t["dbgB_gk_d"] = nc.dram_tensor("dbgB_gk", [128, D], F32, kind="ExternalOutput")
        t["dbgB_Gb_d"] = nc.dram_tensor("dbgB_Gb", [128, H * 144], F32, kind="ExternalOutput")
        for wn in ("wq", "wk", "wv", "wo"):
            t[f"dbgB_{wn}_d"] = nc.dram_tensor(f"dbgB_{wn}", [128, 8 * D], F32,
                                               kind="ExternalOutput")
    if DEBUG_TAP is not None:
        for n, shp in (("dbg_qn", [128, D]), ("dbg_kn", [128, D]),
                       ("dbg_vb", [128, D]), ("dbg_en", [128, 2048]),
                       ("dbg_rs", [128, 8]), ("dbg_afm", [128, D]),
                       ("dbg_xn", [128, D]), ("dbg_xnT", [128, D]),
                       ("dbg_eall", [128, H * 144])):
            t[n + "_d"] = nc.dram_tensor(n, shp, F32, kind="ExternalOutput")

    perm = ExitStack()

    def sb(name, shape, dtype, stk=perm):
        return stk.enter_context(nc.sbuf_tensor(name, shape, dtype))

    t["xres"] = sb("xres", [128, TPS * 1024], F32)
    t["b1f"] = sb("b1f", [128, 32], F32)
    t["b1tmp"] = sb("b1tmp", [32, 128], BF16)
    t["b1T"] = sb("b1T", [128, 32], BF16)
    t["b2row"] = sb("b2row", [1, D], BF16)
    t["onesb"] = sb("onesb", [1, 128], BF16)
    t["idb"] = sb("idb", [128, 128], BF16)
    for n in ("xsumA", "xsqA", "muA", "rstdA", "vtmpA", "vtmp2A", "xsqAe"):
        t[n] = sb(n, [128, TPS], F32)
    for n in ("xsumB", "xsqB", "muB", "ex2B", "mu2B", "varB", "lnB", "rstdB",
              "qssB", "kssB", "rqB", "rkB", "lnB2", "msB", "xsqBe"):
        t[n] = sb(n, [128, 2], F32)
    t["ssqe"] = sb("ssqe", [128, 4], F32)
    t["negm"] = sb("negm", [128, 16], F32)
    t["negam"] = sb("negam", [128, 16], F32)
    t["rs8"] = sb("rs8", [128, 8], F32)
    t["rr8"] = sb("rr8", [128, 8], F32)

    stkA = ExitStack()
    t["w1b"] = sb("w1b", [128, 8 * HID], BF16, stkA)
    t["w2b"] = sb("w2b", [128, 32 * D], BF16, stkA)
    t["xng"] = sb("xng", [128, 2048], BF16, stkA)
    t["xnTg"] = sb("xnTg", [128, 2048], BF16, stkA)
    t["hbuf"] = sb("hbuf", [128, 3 * 256], BF16, stkA)
    stkA.close()

    stkB = ExitStack()
    for n in ("wqT", "wkT", "wvT", "woT", "wstg"):
        t[n] = sb(n, [128, 8 * D], BF16, stkB)
    t["Gb"] = sb("Gb", [128, H * 144], BF16, stkB)
    t["gq_rep"] = sb("gq_rep", [128, D], F32, stkB)
    t["gk_rep"] = sb("gk_rep", [128, D], F32, stkB)
    for n in ("xnB", "qn", "qnT", "kn", "vb0", "vb1", "attn_fm", "junkB"):
        t[n] = sb(n, [128, D], BF16, stkB)
    t["xnTB"] = sb("xnTB", [128, 2 * D], BF16, stkB)
    t["knT"] = sb("knT", [128, 2 * D], BF16, stkB)
    t["e_all"] = sb("e_all", [128, H * 144], BF16, stkB)
    if DEBUG_TAP is not None or DEBUG_TAPB:
        t["dbg_stg"] = sb("dbg_stg", [128, 2048], F32, stkB)
    t["e_n"] = sb("e_n", [128, H * 256], BF16, stkB)
    t["eT"] = sb("eT", [128, H * 256], BF16, stkB)
    stkB.close()

    stkP0 = ExitStack()
    t["pst"] = [stkP0.enter_context(nc.psum_tensor(f"pst{i}", [128, 512], BF16))
                for i in range(2)]
    stkP0.close()
    stkPA = ExitStack()
    t["psh"] = [stkPA.enter_context(nc.psum_tensor(f"psh{i}", [128, 512], F32))
                for i in range(2)]
    t["pso"] = stkPA.enter_context(nc.psum_tensor("pso", [128, 2048], F32))
    stkPA.close()
    stkPB = ExitStack()
    for n in ("psq", "psk", "psv", "psx"):
        t[n] = stkPB.enter_context(nc.psum_tensor(n, [128, 1024], F32))
    stkPB.close()

    with ExitStack() as ctx:
        sems = {}
        for s in ("dW", "dX", "dT", "dS", "cV", "cS", "cP", "cB"):
            sems[s] = ctx.enter_context(nc.semaphore(s))
        block = ctx.enter_context(nc.Block())

        @block.sync
        def _(eng):
            build_walk(Walk("sync", eng, sems), t)

        @block.gpsimd
        def _(eng):
            build_walk(Walk("gpsimd", eng, sems), t)

        @block.tensor
        def _(eng):
            build_walk(Walk("tensor", eng, sems), t)

        @block.vector
        def _(eng):
            build_walk(Walk("vector", eng, sems), t)

        @block.scalar
        def _(eng):
            build_walk(Walk("scalar", eng, sems), t)

    perm.close()
    return nc


_NC_CACHE = None


def _get_nc():
    global _NC_CACHE
    if _NC_CACHE is None:
        _NC_CACHE = build_nc()
    return _NC_CACHE


def _build_G():
    slopes = np.exp2(-np.arange(1, H + 1, dtype=np.float64))
    G = np.zeros((128, H * 144), np.float32)
    i = np.arange(128)[:, None]
    j = np.arange(144)[None, :]
    band = (j >= i) & (j <= i + WIN)
    for h in range(H):
        G[:, h * 144:(h + 1) * 144] = np.where(
            band, np.exp(slopes[h] * (j - WIN - i)), 0.0).astype(np.float32)
    return G


_EYE = np.eye(128, dtype=np.float32)


def kernel(L0_attn_x, L0_attn_wq, L0_attn_wk, L0_attn_wv, L0_attn_wo,
           L0_attn_q_norm_w, L0_attn_k_norm_w,
           L0_ffn_w1, L0_ffn_b1, L0_ffn_w2, L0_ffn_b2):
    x = np.ascontiguousarray(np.asarray(L0_attn_x, np.float32))
    wq = np.ascontiguousarray(np.asarray(L0_attn_wq, np.float32))
    wk = np.ascontiguousarray(np.asarray(L0_attn_wk, np.float32))
    wv = np.ascontiguousarray(np.asarray(L0_attn_wv, np.float32))
    wo = np.ascontiguousarray(np.asarray(L0_attn_wo, np.float32))
    w1 = np.ascontiguousarray(np.asarray(L0_ffn_w1, np.float32))
    w2 = np.ascontiguousarray(np.asarray(L0_ffn_w2, np.float32))
    b1 = np.ascontiguousarray(np.asarray(L0_ffn_b1, np.float32).reshape(32, 128))
    b2 = np.ascontiguousarray(np.asarray(L0_ffn_b2, np.float32).reshape(1, D))
    gq = np.ascontiguousarray(np.asarray(L0_attn_q_norm_w, np.float32).reshape(1, D))
    gk = np.ascontiguousarray(np.asarray(L0_attn_k_norm_w, np.float32).reshape(1, D))
    G = _build_G()

    nc = _get_nc()
    in_maps = []
    for c in range(NCORES):
        in_maps.append(dict(
            x=np.ascontiguousarray(x[2 * c:2 * c + 2]),
            wq=wq, wk=wk, wv=wv, wo=wo, w1=w1, w2=w2,
            b1r=b1, b2r=b2, gq=gq, gk=gk, G=G, eye=_EYE))
    res = run_bass_kernel_spmd(nc, in_maps, core_ids=list(range(NCORES)))
    out = np.concatenate([np.asarray(r["out"]) for r in res.results], axis=0)
    return out.astype(np.float32)
